# version 04
# baseline (speedup 1.0000x reference)
import numpy as np
import ml_dtypes
import concourse.bacc as bacc
import concourse.bass as bass
import concourse.mybir as mybir
from concourse.bass_utils import run_bass_kernel_spmd
from concourse.library_config import sparse_gather as sg_lib, mlp as mlp_lib
from contextlib import ExitStack

F32 = mybir.dt.float32
BF16 = mybir.dt.bfloat16
U32 = mybir.dt.uint32
I16 = mybir.dt.int16
AL = mybir.AluOpType
AX = mybir.AxisListType
AF = mybir.ActivationFunctionType

T = 4096
D = 1024
NH = 16
CH = 128
NCH = T // CH
NT = 32
NE = 8
KSEL = 1024
FF = 4096
OH = 512
NIT = 28

DEBUG = False
STAGE = 9
P5LV = 3
P5NE = 8


def build():
    nc = bacc.Bacc("TRN2")

    xd = nc.dram_tensor("x", [T, D], F32, kind="ExternalInput").ap()
    wwr = nc.dram_tensor("wwr", [D, D], F32, kind="ExternalInput").ap()
    wrd = nc.dram_tensor("wrd", [D, D], F32, kind="ExternalInput").ap()
    decay = nc.dram_tensor("decay", [NH], F32, kind="ExternalInput").ap()
    logal = nc.dram_tensor("logal", [NH], F32, kind="ExternalInput").ap()
    gatew = nc.dram_tensor("gatew", [NE, D], F32, kind="ExternalInput").ap()
    w1d = nc.dram_tensor("w1", [NE, D, FF], F32, kind="ExternalInput").ap()
    w2d = nc.dram_tensor("w2", [NE, FF, OH], F32, kind="ExternalInput").ap()
    cvec = nc.dram_tensor("cvec", [D], F32, kind="ExternalInput").ap()
    wmodd = nc.dram_tensor("wmod", [D, 6 * D], F32, kind="ExternalInput").ap()
    identd = nc.dram_tensor("identc", [128, 128], BF16, kind="ExternalInput").ap()
    identf = nc.dram_tensor("identf", [128, 128], F32, kind="ExternalInput").ap()
    dtile = nc.dram_tensor("dtile", [128, 128], F32, kind="ExternalInput").ap()
    caus = nc.dram_tensor("caus", [128, 128], F32, kind="ExternalInput").ap()
    iotar = nc.dram_tensor("iotar", [128, 128], F32, kind="ExternalInput").ap()
    iota127 = nc.dram_tensor("iota127", [128, 1], F32, kind="ExternalInput").ap()
    gmat = nc.dram_tensor("gmat", [128, 128], F32, kind="ExternalInput").ap()
    tokp1 = nc.dram_tensor("tokp1", [16, 2048], F32, kind="ExternalInput").ap()

    yout = nc.dram_tensor("y", [T, OH], F32, kind="ExternalOutput").ap()
    ik = "ExternalOutput" if DEBUG else "Internal"
    h1h = nc.dram_tensor("h1h", [T, D], BF16, kind=ik).ap()
    baseh = nc.dram_tensor("baseh", [T, D], F32, kind=ik).ap()
    x2h = nc.dram_tensor("x2h", [T, D], F32, kind=ik).ap()
    h2h = nc.dram_tensor("h2h", [T, D], BF16, kind=ik).ap()
    probsh = nc.dram_tensor("probsh", [NE, T], F32, kind=ik).ap()
    gh = nc.dram_tensor("gh", [NE, KSEL], F32, kind=ik).ap()
    idxh = nc.dram_tensor("idxh", [128, 640], I16, kind=ik).ap()
    scat = nc.dram_tensor("scat", [T, OH], F32, kind=ik).ap()
    tauh = nc.dram_tensor("tauh", [128], F32, kind=ik).ap()
    lgh = nc.dram_tensor("lgh", [NH], F32, kind="Internal").ap()
    alh = nc.dram_tensor("alh", [NH], F32, kind="Internal").ap()
    gch = nc.dram_tensor("gch", [NH], F32, kind="Internal").ap()
    modh = nc.dram_tensor("modh", [6 * D], F32, kind="Internal").ap()
    garh = nc.dram_tensor("garh", [D], F32, kind="Internal").ap()
    gbrh = nc.dram_tensor("gbrh", [D], F32, kind="Internal").ap()
    totsh = nc.dram_tensor("totsh", [16, 8], F32, kind="Internal").ap()
    offsh = nc.dram_tensor("offsh", [8, 16], F32, kind="Internal").ap()
    if DEBUG:
        scth = nc.dram_tensor("scth", [128, NT, 8], F32, kind="ExternalOutput").ap()
        pth = nc.dram_tensor("pth", [128, NT, 8], F32, kind="ExternalOutput").ap()
        h2td = nc.dram_tensor("h2td", [128, 8, 128], BF16, kind="ExternalOutput").ap()

    st = ExitStack()
    E = st.enter_context

    ident = E(nc.sbuf_tensor("ident", [128, 128], BF16))
    identfs = E(nc.sbuf_tensor("identfs", [128, 128], F32))
    gwT = E(nc.sbuf_tensor("gwT", [128, 8, NE], BF16))
    gbt = E(nc.sbuf_tensor("gbt", [128, OH], F32))
    iota127l = E(nc.sbuf_tensor("iota127l", [128, 1], F32))
    gml = E(nc.sbuf_tensor("gml", [128, 128], F32))
    d16 = E(nc.sbuf_tensor("d16", [16, 4], F32))
    lgb = E(nc.sbuf_tensor("lgb", [128, 16], F32))
    lgp = E(nc.sbuf_tensor("lgp", [128, 8], F32))
    c128 = E(nc.sbuf_tensor("c128", [128, 8], F32))
    silc = E(nc.sbuf_tensor("silc", [128, 8], BF16))
    mod_sb = E(nc.sbuf_tensor("mod_sb", [128, 48], F32))
    acol = E(nc.sbuf_tensor("acol", [128, 8], F32))
    zero512 = E(nc.sbuf_tensor("zero512", [128, 512], F32))
    stats = E(nc.sbuf_tensor("statsb", [128, NT, 8], F32))
    stats2 = E(nc.sbuf_tensor("stats2b", [128, NT, 8], F32))
    gcol = E(nc.sbuf_tensor("gcol", [128, NE, 8], F32))
    idx32 = E(nc.sbuf_tensor("idx32r", [128, 640], I16))
    scrD = E(nc.sbuf_tensor("scrD", [128, D], F32))
    scrD2 = E(nc.sbuf_tensor("scrD2", [128, D], F32))
    ph02 = ExitStack()
    E02 = ph02.enter_context
    wwT = E02(nc.sbuf_tensor("wwT", [128, 8, D], BF16))
    wrT = E02(nc.sbuf_tensor("wrT", [128, 8, D], BF16))
    mtile = E02(nc.sbuf_tensor("mtile", [128, 16, 128], BF16))
    gppair = E02(nc.sbuf_tensor("gppair", [128, 8, 128], BF16))
    gwcols = E02(nc.sbuf_tensor("gwcols", [128, 16], BF16))
    gcq = E02(nc.sbuf_tensor("gcq", [128, 8], F32))
    ca1t = E02(nc.sbuf_tensor("ca1t", [128, D], BF16))
    sat = E02(nc.sbuf_tensor("sat", [128, D], BF16))
    gat = E02(nc.sbuf_tensor("gat", [128, D], BF16))
    cb1t = E02(nc.sbuf_tensor("cb1t", [128, D], BF16))
    sbt = E02(nc.sbuf_tensor("sbt", [128, D], BF16))

    sS = E(nc.semaphore("sS"))
    sG = E(nc.semaphore("sG"))
    sV = E(nc.semaphore("sV"))
    sC = E(nc.semaphore("sC"))
    sT = E(nc.semaphore("sT"))
    sP = E(nc.semaphore("sP"))
    sDD = E(nc.semaphore("sDD"))
    sDS = E(nc.semaphore("sDS"))
    sWM = [E(nc.semaphore("sWM0")), E(nc.semaphore("sWM1"))]
    sX = [E(nc.semaphore("sX0")), E(nc.semaphore("sX1"))]
    sH = [E(nc.semaphore("sH0")), E(nc.semaphore("sH1"))]
    sBS = [E(nc.semaphore("sBS0")), E(nc.semaphore("sBS1"))]
    sCH = [E(nc.semaphore("sCH0")), E(nc.semaphore("sCH1"))]
    sX2S = [E(nc.semaphore("sX2S0")), E(nc.semaphore("sX2S1"))]
    sH2S = [E(nc.semaphore("sH2S0")), E(nc.semaphore("sH2S1"))]
    sTT = E(nc.semaphore("sTT"))
    sPR = E(nc.semaphore("sPR"))
    sW1 = [E(nc.semaphore(f"sW1_{i}")) for i in range(4)]
    sW2 = [E(nc.semaphore(f"sW2_{i}")) for i in range(4)]
    sF = [E(nc.semaphore("sF0")), E(nc.semaphore("sF1"))]
    sY = [E(nc.semaphore("sY0")), E(nc.semaphore("sY1"))]


    def _fin(stacks, waits=()):
        if waits:
            with nc.Block() as blk_:
                @blk_.sync
                def _(s):
                    for sem_, val_ in waits:
                        s.wait_ge(sem_, val_)
        for s_ in stacks:
            s_.close()
        nc.compile()
        return nc
    ph0 = ExitStack()
    E0 = ph0.enter_context
    wwn = E0(nc.sbuf_tensor("wwn", [128, 8, D], BF16))
    wrn = E0(nc.sbuf_tensor("wrn", [128, 8, D], BF16))
    gwn = E0(nc.sbuf_tensor("gwn", [8, D], BF16))
    wmt = E0(nc.sbuf_tensor("wmt", [128, 2, 8, 128], BF16))
    dtl = E0(nc.sbuf_tensor("dtl", [128, 128], F32))
    causl = E0(nc.sbuf_tensor("causl", [128, 128], F32))
    iotarl = E0(nc.sbuf_tensor("iotarl", [128, 128], F32))
    mrow = E0(nc.sbuf_tensor("mrow", [128, 16, 128], F32))
    rows4 = E0(nc.sbuf_tensor("rows4", [1, 4, D], F32))
    rowb = rows4[0:1, 0]
    rowb2 = rows4[0:1, 1]
    rowb3 = rows4[0:1, 2]
    prow = rows4[0:1, 3]
    pmod = E0(nc.psum_tensor("pmod", [128, 48], F32))
    ptrA = E0(nc.psum_tensor("ptrA", [128, 128], BF16))
    ptrB = E0(nc.psum_tensor("ptrB", [128, 128], BF16))

    S0_CONSTS = 16 * 10
    S0_GAM = S0_CONSTS + 32
    S0_LGB = S0_GAM + 48
    S0_AL = S0_LGB + 32
    S0_ACOL = S0_AL + 64
    S0_MODH = S0_ACOL + 16
    S0_ROWS = S0_MODH + 32
    S0_GAGB = S0_ROWS + 32
    S0_END = S0_GAGB
    G0_WR = 16 * 3
    G0_ZERO = G0_WR + 16
    G0_TILES = G0_ZERO + 16 * 7
    G0_END = G0_TILES
    C0_GAM = 1
    C0_TABLES = 2
    T0_MOD = 48
    T0_TRANS = T0_MOD + 136
    V0_MT = 1
    V0_EVAC = V0_MT + 136
    V0_MODEV = V0_EVAC + 1
    V0_CLAMP = V0_MODEV + 1
    V0_TILES = V0_CLAMP + 1
    V0_END = V0_TILES

    with nc.Block() as blk:
        @blk.sync
        def _(s):
            s.dma_start(ident[:], identd).then_inc(sS, 16)
            s.dma_start(identfs[:], identf).then_inc(sS, 16)
            s.dma_start(dtl[:], dtile).then_inc(sS, 16)
            s.dma_start(causl[:], caus).then_inc(sS, 16)
            s.dma_start(iotarl[:], iotar).then_inc(sS, 16)
            s.dma_start(iota127l[:], iota127).then_inc(sS, 16)
            s.dma_start(gml[:], gmat).then_inc(sS, 16)
            s.dma_start(d16[:, 0:1], decay.rearrange("(h one) -> h one", one=1)).then_inc(sS, 16)
            s.dma_start(c128[:], cvec.rearrange("(k p) -> p k", p=128),
                        allow_slow_non_contiguous=True).then_inc(sS, 16)
            s.dma_start(d16[:, 1:2], logal.rearrange("(h one) -> h one", one=1)).then_inc(sS, 16)

        @blk.gpsimd
        def _(g):
            g.dma_start(wwn[:], wwr.rearrange("(ic ip) k -> ip ic k", ip=128)).then_inc(sG, 16)
            g.dma_start(wrn[:], wrd.rearrange("(oc op) f -> op oc f", op=128)).then_inc(sG, 16)
            g.dma_start(gwn[:], gatew).then_inc(sG, 16)
            g.wait_ge(sG, 48)
            g.memset(zero512[:], 0.0)
            g.drain()
            g.dma_start(scat.rearrange("(t p) c -> p t c", p=128),
                        zero512[:].unsqueeze(1).broadcast_to([128, NT, 512])).then_inc(sG, 16)
            g.wait_ge(sG, 64)
            for cc in range(48):
                if cc >= 2:
                    g.wait_ge(sT, cc - 1)
                g.dma_start(wmt[:, cc % 2], wmodd.rearrange("(kc kp) f -> kp kc f", kp=128)
                            [:, :, 128 * cc:128 * (cc + 1)]).then_inc(sWM[cc % 2], 16)

        @blk.scalar
        def _(sc):
            sc.wait_ge(sS, S0_CONSTS)
            sc.activation(d16[:, 2:3], d16[:, 0:1], AF.Sigmoid)
            sc.drain()
            sc.activation(d16[:, 3:4], d16[:, 2:3], AF.Ln)
            sc.activation(silc[:], c128[:], AF.Silu).then_inc(sC, 1)
            sc.wait_ge(sS, S0_LGB)
            sc.activation(d16[:, 0:1], d16[:, 1:2], AF.Exp)
            sc.activation(acol[0:16, 0:1], d16[:, 3:4], AF.Exp, scale=float(CH))
            for h in range(16):
                sc.activation(mrow[:, h, :], dtl[:], AF.Exp, scale=lgb[:, h:h + 1])
            for hp in range(8):
                sc.activation(gppair[:, hp, :], iotarl[:], AF.Exp, scale=lgp[:, hp:hp + 1])
            for h in range(16):
                ins = sc.activation(gwcols[:, h:h + 1], iota127l[:], AF.Exp, scale=lgb[:, h:h + 1])
            ins.then_inc(sC, 1)

        @blk.sync
        def _(s):
            s.wait_ge(sC, C0_GAM)
            s.dma_start(lgh.rearrange("(h one) -> h one", one=1), d16[:, 3:4]).then_inc(sS, 16)
            s.dma_start(alh.rearrange("(h one) -> h one", one=1), d16[:, 3:4]).then_inc(sS, 16)
            s.wait_ge(sS, S0_GAM)
            s.dma_start(lgb[:], bass.AP(lgh.tensor, 0, [[0, 128], [1, 16]]),
                        allow_slow_non_contiguous=True).then_inc(sS, 16)
            for h2_ in range(2):
                s.dma_start(lgp[64 * h2_:64 * (h2_ + 1), :],
                            bass.AP(lgh.tensor, h2_, [[0, 64], [2, 8]]),
                            allow_slow_non_contiguous=True).then_inc(sS, 16)
            s.wait_ge(sC, C0_TABLES)
            s.dma_start(alh.rearrange("(h one) -> h one", one=1), d16[:, 0:1]).then_inc(sS, 16)
            s.dma_start(gch.rearrange("(h one) -> h one", one=1), acol[0:16, 0:1]).then_inc(sS, 16)
            s.wait_ge(sS, S0_AL)
            for h2_ in range(2):
                s.dma_start(acol[64 * h2_:64 * (h2_ + 1), :],
                            bass.AP(alh.tensor, h2_, [[0, 64], [2, 8]]),
                            allow_slow_non_contiguous=True).then_inc(sS, 16)
                s.dma_start(gcq[64 * h2_:64 * (h2_ + 1), :],
                            bass.AP(gch.tensor, h2_, [[0, 64], [2, 8]]),
                            allow_slow_non_contiguous=True).then_inc(sS, 16)

        @blk.vector
        def _(v):
            v.wait_ge(sC, C0_TABLES)
            for h in range(16):
                ins = v.tensor_mul(mtile[:, h, :], mrow[:, h, :], causl[:])
            ins.then_inc(sV, 1)

        @blk.tensor
        def _(t):
            t.wait_ge(sC, C0_GAM)
            for cc in range(48):
                t.wait_ge(sWM[cc % 2], 16 * (cc // 2 + 1))
                for kc in range(8):
                    ins = t.matmul(pmod[:, cc:cc + 1], wmt[:, cc % 2, kc, :], silc[:, kc:kc + 1],
                                   start=(kc == 0), stop=(kc == 7))
                ins.then_inc(sT, 1)
            ntr = 0
            for src in (wwn, wrn):
                for oc in range(8):
                    for kc in range(8):
                        if ntr >= 2:
                            t.wait_ge(sV, V0_MT + ntr - 1)
                        dst = ptrA if ntr % 2 == 0 else ptrB
                        t.transpose(dst[:, :], src[:, oc, 128 * kc:128 * (kc + 1)], ident[:]
                                    ).then_inc(sT, 1)
                        ntr += 1
            for kc in range(8):
                t.wait_ge(sV, V0_MT + ntr - 1)
                dst = ptrA if ntr % 2 == 0 else ptrB
                t.transpose(dst[0:128, 0:8], gwn[:, 128 * kc:128 * (kc + 1)], ident[0:8, 0:8]
                            ).then_inc(sT, 1)
                ntr += 1

        @blk.vector
        def _(v):
            v.wait_ge(sS, S0_ACOL)
            ntr = 0
            for which in ("ww", "wr"):
                for oc in range(8):
                    for kc in range(8):
                        v.wait_ge(sT, T0_MOD + ntr + 1)
                        src = ptrA if ntr % 2 == 0 else ptrB
                        if which == "ww":
                            v.tensor_copy(wwT[:, kc, 128 * oc:128 * (oc + 1)], src[:, :]
                                          ).then_inc(sV, 1)
                        else:
                            v.tensor_scalar(wrT[:, kc, 128 * oc:128 * (oc + 1)], src[:, :],
                                            acol[:, kc:kc + 1], None, AL.mult).then_inc(sV, 1)
                        ntr += 1
            for kc in range(8):
                v.wait_ge(sT, T0_MOD + ntr + 1)
                src = ptrA if ntr % 2 == 0 else ptrB
                v.tensor_copy(gwT[:, kc, :], src[0:128, 0:8]).then_inc(sV, 1)
                ntr += 1
            v.wait_ge(sT, T0_MOD)
            v.tensor_copy(mod_sb[:], pmod[:]).then_inc(sV, 1)

        @blk.sync
        def _(s):
            s.wait_ge(sV, V0_MODEV)
            s.dma_start(modh.rearrange("(c p) -> p c", p=128), mod_sb[:],
                        allow_slow_non_contiguous=True).then_inc(sS, 16)
            s.wait_ge(sS, S0_MODH)
            s.dma_start(rowb, modh[2 * D:3 * D].rearrange("(one d) -> one d", one=1)
                        ).then_inc(sS, 16)
            s.dma_start(rowb2, modh[5 * D:6 * D].rearrange("(one d) -> one d", one=1)
                        ).then_inc(sS, 16)

        @blk.vector
        def _(v):
            v.wait_ge(sS, S0_ROWS)
            for rb in (rowb, rowb2):
                v.drain()
                v.tensor_scalar(prow, rb, 0.0, None, AL.is_ge)
                v.drain()
                v.tensor_scalar(prow, prow, 2.0, 1.0, AL.mult, op1=AL.subtract)
                v.tensor_scalar(rowb3, rb, -1.0, None, AL.mult)
                v.drain()
                v.tensor_max(rowb3, rowb3, rb)
                v.drain()
                v.tensor_scalar_max(rowb3, rowb3, 0.25)
                v.drain()
                ins = v.tensor_mul(rb, prow, rowb3)
            ins.then_inc(sV, 1)

        @blk.sync
        def _(s):
            s.wait_ge(sV, V0_CLAMP)
            s.dma_start(garh.rearrange("(one d) -> one d", one=1), rowb).then_inc(sS, 16)
            s.dma_start(gbrh.rearrange("(one d) -> one d", one=1), rowb2).then_inc(sS, 16)

        @blk.gpsimd
        def _(g):
            g.wait_ge(sS, S0_GAGB)
            g.dma_start(sat[:], bass.AP(modh.tensor, 0 * D, [[0, 128], [1, D]]),
                        allow_slow_non_contiguous=True).then_inc(sG, 16)
            g.dma_start(ca1t[:], bass.AP(modh.tensor, 1 * D, [[0, 128], [1, D]]),
                        allow_slow_non_contiguous=True).then_inc(sG, 16)
            g.dma_start(sbt[:], bass.AP(modh.tensor, 3 * D, [[0, 128], [1, D]]),
                        allow_slow_non_contiguous=True).then_inc(sG, 16)
            g.dma_start(cb1t[:], bass.AP(modh.tensor, 4 * D, [[0, 128], [1, D]]),
                        allow_slow_non_contiguous=True).then_inc(sG, 16)
            g.dma_start(gat[:], bass.AP(garh.tensor, 0, [[0, 128], [1, D]]),
                        allow_slow_non_contiguous=True).then_inc(sG, 16)
            g.dma_start(gbt[:], bass.AP(gbrh.tensor, 0, [[0, 128], [1, OH]]),
                        allow_slow_non_contiguous=True).then_inc(sG, 16)
            g.dma_start(scrD[0:1, :], bass.AP(garh.tensor, 0, [[0, 1], [1, D]]),
                        allow_slow_non_contiguous=True).then_inc(sG, 16)

        @blk.vector
        def _(v):
            v.wait_ge(sG, G0_TILES)
            v.tensor_scalar(ca1t[:], ca1t[:], 1.0, None, AL.add)
            v.tensor_scalar(cb1t[:], cb1t[:], 1.0, None, AL.add)
            for kc in range(8):
                ins = v.tensor_mul(wrT[:, kc, :], wrT[:, kc, :], gat[:, :])
            ins.then_inc(sV, 1)

    ph0.close()
    if STAGE < 1:
        return _fin([ph02, st])

    ph12 = ExitStack()
    E1 = ph12.enter_context
    h1T = E1(nc.sbuf_tensor("h1T", [128, 8, 64 + T], BF16))
    xt = E1(nc.sbuf_tensor("xt", [128, 2, D], F32))
    ut = E1(nc.sbuf_tensor("ut", [128, 2, D], BF16))
    h1t_ = E1(nc.sbuf_tensor("h1t_", [128, 2, D], BF16))
    tmpt = E1(nc.sbuf_tensor("tmpt", [128, 2, D], BF16))
    baset = E1(nc.sbuf_tensor("baset", [128, 2, D], F32))
    ph1ps = ExitStack()
    pt1 = ph1ps.enter_context(nc.psum_tensor("pt1", [128, 128], BF16))
    pt2 = ph1ps.enter_context(nc.psum_tensor("pt2", [128, 128], BF16))

    SX_A = lambda t_: 16 * (t_ // 2 + 1)
    SX_B = lambda t_: 16 * (NT // 2 + t_ // 2 + 1)
    SX_3 = lambda t_: 16 * (NT + t_ // 2 + 1)
    V1_ST = lambda t_: V0_END + (t_ + 1)
    V1_CHAIN = V0_END + NT + 1
    C1_SQRT = C0_TABLES + 1
    V1_RSTD = V1_CHAIN + 1
    SH_ = lambda t_: 16 * (t_ // 2 + 1)
    S1_END = S0_END
    C1_U = lambda t_: C1_SQRT + t_ + 1
    C1_END = C1_SQRT + NT
    _v1_h1, _v1_base, _v1_ev = {}, {}, {}
    _tok = 0
    for _t in range(NT):
        _tok += 1; _v1_h1[_t] = _tok
        _tok += 1; _v1_base[_t] = _tok
        if _t >= 1:
            for _kc in range(8):
                _tok += 1; _v1_ev[(_t - 1, _kc)] = _tok
    for _kc in range(8):
        _tok += 1; _v1_ev[(NT - 1, _kc)] = _tok
    _tok += 1
    _V1_TOTAL = _tok
    V1_H1 = lambda t_: V1_RSTD + _v1_h1[t_]
    V1_BASE = lambda t_: V1_RSTD + _v1_base[t_]
    V1_EV = lambda t_, kc: V1_RSTD + _v1_ev[(t_, kc)]
    T1_TR = lambda t_, kc: T0_TRANS + 8 * t_ + kc + 1
    T1_END = T0_TRANS + 8 * NT
    V1_PAD = V1_RSTD + _V1_TOTAL
    V1_END = V1_PAD

    with nc.Block() as blk:
        @blk.sync
        def _(s):
            for t_ in range(NT):
                if t_ >= 2:
                    s.wait_ge(sV, V1_ST(t_ - 2))
                s.dma_start(xt[:, t_ % 2], xd.rearrange("(n p) d -> n p d", p=128)[t_]
                            ).then_inc(sX[t_ % 2], 16)
            for t_ in range(NT):
                if t_ >= 2:
                    s.wait_ge(sV, V1_BASE(t_ - 2))
                elif t_ == 0:
                    s.wait_ge(sV, V1_ST(NT - 2))
                else:
                    s.wait_ge(sV, V1_ST(NT - 1))
                s.dma_start(xt[:, t_ % 2], xd.rearrange("(n p) d -> n p d", p=128)[t_]
                            ).then_inc(sX[t_ % 2], 16)
                s.wait_ge(sV, V1_H1(t_))
                s.dma_start(h1h.rearrange("(n p) d -> n p d", p=128)[t_], h1t_[:, t_ % 2]
                            ).then_inc(sH[t_ % 2], 16)
                s.wait_ge(sV, V1_BASE(t_))
                s.dma_start(baseh.rearrange("(n p) d -> n p d", p=128)[t_], baset[:, t_ % 2]
                            ).then_inc(sBS[t_ % 2], 16)

        @blk.vector
        def _(v):
            for t_ in range(NT):
                v.wait_ge(sX[t_ % 2], SX_A(t_))
                v.reduce_sum(stats[:, t_, 0:1], xt[:, t_ % 2], axis=AX.X)
                v.tensor_tensor_reduce(scrD[:, :] if t_ % 2 == 0 else scrD2[:, :],
                                       xt[:, t_ % 2], xt[:, t_ % 2], 1.0, 0.0,
                                       AL.mult, AL.add, accum_out=stats[:, t_, 1:2]
                                       ).then_inc(sV, 1)
            v.drain()
            v.tensor_scalar(stats[:, :, 2:3], stats[:, :, 0:1], 1.0 / D, None, AL.mult)
            v.drain()
            v.tensor_scalar(stats[:, :, 3:4], stats[:, :, 1:2], 1.0 / D, None, AL.mult)
            v.tensor_mul(stats[:, :, 4:5], stats[:, :, 2:3], stats[:, :, 2:3])
            v.drain()
            v.tensor_sub(stats[:, :, 3:4], stats[:, :, 3:4], stats[:, :, 4:5])
            v.drain()
            v.tensor_scalar(stats[:, :, 5:6], stats[:, :, 3:4], 1e-5, None, AL.add
                            ).then_inc(sV, 1)

        @blk.scalar
        def _(sc):
            sc.wait_ge(sV, V1_CHAIN)
            sc.activation(stats[:, :, 6:7], stats[:, :, 5:6], AF.Sqrt).then_inc(sC, 1)

        @blk.vector
        def _(v):
            v.wait_ge(sC, C1_SQRT)
            v.reciprocal(stats[:, :, 7:8], stats[:, :, 6:7])
            v.drain()
            v.tensor_mul(stats[:, :, 4:5], stats[:, :, 2:3], stats[:, :, 7:8])
            v.drain()
            v.tensor_scalar(stats[:, :, 4:5], stats[:, :, 4:5], -1.0, None, AL.mult
                            ).then_inc(sV, 1)

        @blk.scalar
        def _(sc):
            sc.wait_ge(sV, V1_RSTD)
            for t_ in range(NT):
                sc.wait_ge(sX[t_ % 2], SX_B(t_))
                if t_ >= 2:
                    sc.wait_ge(sV, V1_H1(t_ - 2))
                sc.activation(ut[:, t_ % 2], xt[:, t_ % 2], AF.Identity,
                              bias=stats[:, t_, 4:5], scale=stats[:, t_, 7:8]
                              ).then_inc(sC, 1)

        @blk.vector
        def _(v):
            def evacs(tt):
                for kc in range(8):
                    j = 8 * tt + kc
                    v.wait_ge(sT, T1_TR(tt, kc))
                    src = pt1 if j % 2 == 0 else pt2
                    v.tensor_copy(h1T[:, kc, 64 + 128 * tt:64 + 128 * (tt + 1)], src[:, :]
                                  ).then_inc(sV, 1)
            for t_ in range(NT):
                i = t_ % 2
                v.wait_ge(sC, C1_U(t_))
                if t_ >= 2:
                    v.wait_ge(sT, T1_TR(t_ - 2, 7))
                    v.wait_ge(sH[t_ % 2], SH_(t_ - 2))
                v.tensor_mul(h1t_[:, i], ut[:, i], ca1t[:])
                v.drain()
                v.tensor_add(h1t_[:, i], h1t_[:, i], sat[:]).then_inc(sV, 1)
                v.drain()
                v.tensor_mul(tmpt[:, i], h1t_[:, i], gat[:])
                v.drain()
                if t_ >= 2:
                    v.wait_ge(sBS[t_ % 2], 16 * (t_ // 2))
                sd = scrD if i == 0 else scrD2
                v.tensor_scalar(sd[:, :], tmpt[:, i], 1.0, None, AL.mult)
                v.drain()
                v.tensor_add(baset[:, i], sd[:, :], xt[:, i]).then_inc(sV, 1)
                if t_ >= 1:
                    evacs(t_ - 1)
            evacs(NT - 1)
            v.memset(h1T[:, :, 0:64], 0.0).then_inc(sV, 1)

        @blk.tensor
        def _(t):
            for t_ in range(NT):
                t.wait_ge(sV, V1_H1(t_))
                for kc in range(8):
                    j = 8 * t_ + kc
                    if j >= 2:
                        tt2, kc2 = (j - 2) // 8, (j - 2) % 8
                        t.wait_ge(sV, V1_EV(tt2, kc2))
                    dst = pt1 if j % 2 == 0 else pt2
                    t.transpose(dst[:, :], h1t_[:, t_ % 2, 128 * kc:128 * (kc + 1)], ident[:]
                                ).then_inc(sT, 1)

    ph1ps.close()
    if STAGE < 2:
        return _fin([ph12, ph02, st])

    wkb = E1(nc.sbuf_tensor("wkb", [128, 2, D], BF16))
    vcb = E1(nc.sbuf_tensor("vcb", [128, 2, D], BF16))
    vgwb = E1(nc.sbuf_tensor("vgwb", [128, 2, D], BF16))
    rkpb = E1(nc.sbuf_tensor("rkpb", [128, 8, 128], BF16))
    ssb = E1(nc.sbuf_tensor("ssb", [128, 8, 128], BF16))
    rdsb = E1(nc.sbuf_tensor("rdsb", [128, 2, D], BF16))
    stf = E1(nc.sbuf_tensor("stf", [128, 512], F32))
    stb = E1(nc.sbuf_tensor("stb", [128, 512], BF16))
    x2t = E1(nc.sbuf_tensor("x2t", [128, 2, D], F32))
    ps_v = E1(nc.psum_tensor("ps_v", [128, 1024], F32))
    ps_s = E1(nc.psum_tensor("ps_s", [128, 1024], F32))
    ps_r = E1(nc.psum_tensor("ps_r", [128, 1024], F32))
    ps_u = E1(nc.psum_tensor("ps_u", [128, 512], F32))
    ps_p = E1(nc.psum_tensor("ps_p", [128, 512], F32))

    VD = 10
    TD = 8
    V2_0 = V1_END + 1
    V2 = lambda c, k: V2_0 + VD * c + k
    T2 = lambda c, k: T1_END + TD * c + k
    SCH_ = lambda c: 32 * (c // 2 + 1)
    S2_END = S1_END
    P2_MEMSET = 1
    V2_END = V2_0 + VD * NCH
    T2_END = T1_END + TD * NCH

    with nc.Block() as blk:
        @blk.gpsimd
        def _(g):
            g.wait_ge(sH[0], 16 * (NT // 2))
            g.wait_ge(sH[1], 16 * (NT // 2))
            g.memset(wkb[0:1, 0, :], 0.0).then_inc(sP, 1)

        @blk.vector
        def _(v):
            v.memset(stf[:], 0.0)
            v.memset(stb[:], 0.0).then_inc(sV, 1)

        @blk.sync
        def _(s):
            s.wait_ge(sH[0], 16 * (NT // 2))
            s.wait_ge(sH[1], 16 * (NT // 2))
            s.wait_ge(sBS[0], 16 * (NT // 2))
            s.wait_ge(sBS[1], 16 * (NT // 2))
            for c in range(NCH):
                if c >= 2:
                    s.wait_ge(sV, V2(c - 2, 10))
                if c == 0:
                    s.dma_start(wkb[1:128, 0, :], bass.AP(h1h.tensor, 0, [[D, 127], [1, D]])
                                ).then_inc(sCH[0], 16)
                else:
                    s.dma_start(wkb[:, c % 2, :],
                                bass.AP(h1h.tensor, (c * CH - 1) * D, [[D, 128], [1, D]])
                                ).then_inc(sCH[c % 2], 16)
                s.dma_start(baset[:, c % 2], baseh.rearrange("(n p) d -> n p d", p=128)[c]
                            ).then_inc(sCH[c % 2], 16)
                s.wait_ge(sV, V2(c, 10))
                s.dma_start(x2h.rearrange("(n p) d -> n p d", p=128)[c], x2t[:, c % 2]
                            ).then_inc(sX2S[c % 2], 16)

        @blk.tensor
        def _(t):
            t.wait_ge(sV, V1_END)
            t.wait_ge(sP, P2_MEMSET)
            for c in range(NCH):
                i = c % 2
                if c >= 1:
                    t.wait_ge(sV, V2(c - 1, 1))
                for half in range(2):
                    for kc in range(8):
                        ins = t.matmul(ps_v[:, 512 * half:512 * (half + 1)],
                                       h1T[:, kc, 64 + CH * c:64 + CH * (c + 1)],
                                       wwT[:, kc, 512 * half:512 * (half + 1)],
                                       start=(kc == 0), stop=(kc == 7))
                ins.then_inc(sT, 1)
                if c >= 1:
                    t.wait_ge(sV, V2(c - 1, 5))
                for hh_ in (0, 2, 4, 6, -1, 1, 3, 5, 7):
                    if hh_ < 0:
                        t.drain()
                        continue
                    h = hh_
                    par, hp = h % 2, h // 2
                    sl = slice(64 * par, 64 * (par + 1))
                    ins = t.matmul(ps_s[:, 128 * hh_:128 * (hh_ + 1)],
                                   h1T[sl, hp, 64 + CH * c - 1:64 + CH * (c + 1) - 1],
                                   h1T[sl, hp, 64 + CH * c:64 + CH * (c + 1)],
                                   start=True, stop=True)
                ins.then_inc(sT, 1)
                t.wait_ge(sV, V2(c, 4))
                if c >= 1:
                    t.wait_ge(sV, V2(c - 1, 8))
                for hh_ in range(8):
                    h = hh_
                    par, q = h % 2, h // 2
                    t.matmul(ps_r[64 * par:64 * (par + 1), 128 * q:128 * (q + 1)],
                             vcb[:, i, 64 * h:64 * (h + 1)], ssb[:, hh_, :],
                             start=True, stop=False, tile_position=(0, 64 * par))
                    ins = t.matmul(ps_r[64 * par:64 * (par + 1), 128 * q:128 * (q + 1)],
                                   stb[64 * par:64 * (par + 1), 64 * q:64 * (q + 1)],
                                   rkpb[64 * par:64 * (par + 1), q, :],
                                   start=False, stop=True,
                                   tile_position=(64 * par, 64 * par))
                ins.then_inc(sT, 1)
                for hh_ in (0, 2, 4, 6, -1, 1, 3, 5, 7):
                    if hh_ < 0:
                        t.drain()
                        continue
                    h = 8 + hh_
                    par, hp = h % 2, h // 2
                    sl = slice(64 * par, 64 * (par + 1))
                    ins = t.matmul(ps_s[:, 128 * hh_:128 * (hh_ + 1)],
                                   h1T[sl, hp, 64 + CH * c - 1:64 + CH * (c + 1) - 1],
                                   h1T[sl, hp, 64 + CH * c:64 + CH * (c + 1)],
                                   start=True, stop=True)
                ins.then_inc(sT, 1)
                t.wait_ge(sV, V2(c, 5))
                for hh_ in range(8):
                    h = 8 + hh_
                    par, q = h % 2, h // 2
                    t.matmul(ps_r[64 * par:64 * (par + 1), 128 * q:128 * (q + 1)],
                             vcb[:, i, 64 * h:64 * (h + 1)], ssb[:, hh_, :],
                             start=True, stop=False, tile_position=(0, 64 * par))
                    ins = t.matmul(ps_r[64 * par:64 * (par + 1), 128 * q:128 * (q + 1)],
                                   stb[64 * par:64 * (par + 1), 64 * q:64 * (q + 1)],
                                   rkpb[64 * par:64 * (par + 1), q, :],
                                   start=False, stop=True,
                                   tile_position=(64 * par, 64 * par))
                ins.then_inc(sT, 1)
                t.wait_ge(sCH[c % 2], SCH_(c))
                if c >= 1:
                    t.wait_ge(sV, V2(c - 1, 7))
                for h in range(16):
                    par, q = h % 2, h // 2
                    ins = t.matmul(ps_u[64 * par:64 * (par + 1), 64 * q:64 * (q + 1)],
                                   wkb[:, i, 64 * h:64 * (h + 1)],
                                   vgwb[:, i, 64 * h:64 * (h + 1)],
                                   start=True, stop=True, tile_position=(0, 64 * par))
                ins.then_inc(sT, 1)
                t.wait_ge(sV, V2(c, 8))
                if c >= 1:
                    t.wait_ge(sV, V2(c - 1, 10))
                for q in range(8):
                    ins = t.matmul(ps_p[:, :], rdsb[:, i, 128 * q:128 * (q + 1)],
                                   wrT[:, q, 0:512], start=(q == 0), stop=(q == 7))
                ins.then_inc(sT, 1)
                t.wait_ge(sV, V2(c, 9))
                for q in range(8):
                    ins = t.matmul(ps_p[:, :], rdsb[:, i, 128 * q:128 * (q + 1)],
                                   wrT[:, q, 512:1024], start=(q == 0), stop=(q == 7))
                ins.then_inc(sT, 1)

        @blk.vector
        def _(v):
            for c in range(NCH):
                i = c % 2
                v.wait_ge(sT, T2(c, 1))
                v.tensor_copy(vcb[:, i, 0:512], ps_v[:, 0:512])
                v.tensor_copy(vcb[:, i, 512:1024], ps_v[:, 512:1024]).then_inc(sV, 1)
                v.drain()
                v.tensor_mul(vgwb[:, i].rearrange("p (h d) -> p h d", h=16),
                             vcb[:, i].rearrange("p (h d) -> p h d", h=16),
                             gwcols[:, :].unsqueeze(2).broadcast_to([128, 16, 64])
                             ).then_inc(sV, 1)
                for hp in range(8):
                    ins = v.tensor_mul(rkpb[:, hp, :],
                                       h1T[:, hp, 64 + CH * c:64 + CH * (c + 1)],
                                       gppair[:, hp, :])
                ins.then_inc(sV, 1)
                v.wait_ge(sT, T2(c, 2))
                for hh_ in range(8):
                    ins = v.scalar_tensor_tensor(ssb[:, hh_, :],
                                                 ps_s[:, 128 * hh_:128 * (hh_ + 1)],
                                                 1.0, mtile[:, hh_, :], AL.mult, AL.mult)
                ins.then_inc(sV, 1)
                v.wait_ge(sT, T2(c, 4))
                for hh_ in range(8):
                    ins = v.scalar_tensor_tensor(ssb[:, hh_, :],
                                                 ps_s[:, 128 * hh_:128 * (hh_ + 1)],
                                                 1.0, mtile[:, 8 + hh_, :], AL.mult, AL.mult)
                ins.then_inc(sV, 1)
                v.wait_ge(sT, T2(c, 6))
                v.tensor_mul(stf[:].rearrange("p (q d) -> p q d", q=8),
                             stf[:].rearrange("p (q d) -> p q d", q=8),
                             gcq[:, :].unsqueeze(2).broadcast_to([128, 8, 64]))
                v.drain()
                v.tensor_add(stf[:], stf[:], ps_u[:]).then_inc(sV, 1)
                v.drain()
                v.tensor_copy(stb[:], stf[:]).then_inc(sV, 1)
                v.wait_ge(sT, T2(c, 5))
                v.tensor_copy(rdsb[:, i, 0:512], ps_r[:, 0:512])
                v.tensor_copy(rdsb[:, i, 512:1024], ps_r[:, 512:1024]).then_inc(sV, 1)
                v.wait_ge(sT, T2(c, 7))
                v.wait_ge(sCH[c % 2], SCH_(c))
                if c >= 2:
                    v.wait_ge(sX2S[c % 2], 16 * (c // 2))
                v.tensor_add(x2t[:, i, 0:512], baset[:, i, 0:512], ps_p[:, :]
                             ).then_inc(sV, 1)
                v.wait_ge(sT, T2(c, 8))
                v.tensor_add(x2t[:, i, 512:1024], baset[:, i, 512:1024], ps_p[:, :])
                v.drain()
                v.reduce_sum(stats2[:, c, 0:1], x2t[:, i], axis=AX.X)
                v.tensor_tensor_reduce(scrD[:, :] if i == 0 else scrD2[:, :],
                                       x2t[:, i], x2t[:, i], 1.0, 0.0,
                                       AL.mult, AL.add, accum_out=stats2[:, c, 1:2]
                                       ).then_inc(sV, 1)

    if STAGE < 3:
        return _fin([ph12, ph02, st])

    V3_CHAIN = V2_END + 1
    C3_SQRT = C1_END + 1
    V3_RSTD = V3_CHAIN + 1
    S3_END = S2_END
    C3_U = lambda t_: C3_SQRT + t_ + 1
    C3_END = C3_SQRT + NT
    V3_H2 = lambda t_: V3_RSTD + t_ + 1
    V3_H2_END = V3_RSTD + NT

    with nc.Block() as blk:
        @blk.vector
        def _(v):
            v.drain()
            v.tensor_scalar(stats2[:, :, 2:3], stats2[:, :, 0:1], 1.0 / D, None, AL.mult)
            v.drain()
            v.tensor_scalar(stats2[:, :, 3:4], stats2[:, :, 1:2], 1.0 / D, None, AL.mult)
            v.tensor_mul(stats2[:, :, 4:5], stats2[:, :, 2:3], stats2[:, :, 2:3])
            v.drain()
            v.tensor_sub(stats2[:, :, 3:4], stats2[:, :, 3:4], stats2[:, :, 4:5])
            v.drain()
            v.tensor_scalar(stats2[:, :, 5:6], stats2[:, :, 3:4], 1e-5, None, AL.add
                            ).then_inc(sV, 1)

        @blk.scalar
        def _(sc):
            sc.wait_ge(sV, V3_CHAIN)
            sc.activation(stats2[:, :, 6:7], stats2[:, :, 5:6], AF.Sqrt).then_inc(sC, 1)

        @blk.vector
        def _(v):
            v.wait_ge(sC, C3_SQRT)
            v.reciprocal(stats2[:, :, 7:8], stats2[:, :, 6:7])
            v.drain()
            v.tensor_mul(stats2[:, :, 4:5], stats2[:, :, 2:3], stats2[:, :, 7:8])
            v.drain()
            v.tensor_scalar(stats2[:, :, 4:5], stats2[:, :, 4:5], -1.0, None, AL.mult
                            ).then_inc(sV, 1)

        @blk.sync
        def _(s):
            s.wait_ge(sX2S[0], 16 * (NCH // 2))
            s.wait_ge(sX2S[1], 16 * (NCH // 2))
            for t_ in range(NT):
                if t_ >= 2:
                    s.wait_ge(sC, C3_U(t_ - 1))
                s.dma_start(xt[:, t_ % 2], x2h.rearrange("(n p) d -> n p d", p=128)[t_]
                            ).then_inc(sX[t_ % 2], 16)
                s.wait_ge(sV, V3_H2(t_))
                s.dma_start(h2h.rearrange("(n p) d -> n p d", p=128)[t_], h1t_[:, t_ % 2]
                            ).then_inc(sH2S[t_ % 2], 16)

        @blk.scalar
        def _(sc):
            sc.wait_ge(sV, V3_RSTD)
            for t_ in range(NT):
                sc.wait_ge(sX[t_ % 2], SX_3(t_))
                if t_ >= 2:
                    sc.wait_ge(sV, V3_H2(t_ - 2))
                sc.activation(ut[:, t_ % 2], xt[:, t_ % 2], AF.Identity,
                              bias=stats2[:, t_, 4:5], scale=stats2[:, t_, 7:8]
                              ).then_inc(sC, 1)

        @blk.vector
        def _(v):
            for t_ in range(NT):
                i = t_ % 2
                v.wait_ge(sC, C3_U(t_))
                if t_ >= 2:
                    v.wait_ge(sH2S[t_ % 2], 16 * (t_ // 2))
                v.tensor_mul(h1t_[:, i], ut[:, i], cb1t[:])
                v.drain()
                v.tensor_add(h1t_[:, i], h1t_[:, i], sbt[:]).then_inc(sV, 1)

    ph12.close()
    ph02.close()
    if STAGE < 4:
        return _fin([st])

    ph3 = ExitStack()
    E3 = ph3.enter_context
    h2T = E3(nc.sbuf_tensor("h2T", [128, 8, T], BF16))
    sc_tm = E3(nc.sbuf_tensor("sc_tm", [128, NT, 8], F32))
    prob_tm = E3(nc.sbuf_tensor("prob_tm", [128, NT, 8], F32))
    prob_bf = E3(nc.sbuf_tensor("prob_bf", [128, NT, 8], BF16))
    smx = E3(nc.sbuf_tensor("smx", [128, NT, 2], F32))
    probsT = E3(nc.sbuf_tensor("probsT", [8, T], F32))
    ps_sc = E3(nc.psum_tensor("ps_sc", [128, 8], F32))
    ps_tp = E3(nc.psum_tensor("ps_tp", [8, 128], F32))

    S3T_H2T = S3_END
    T3_SC = lambda t_: T2_END + t_ + 1
    T3_SC_END = T2_END + NT
    V3_SCEV = lambda t_: V3_H2_END + t_ + 1
    V3_SM1 = V3_H2_END + NT + 1
    C3_EXP = C3_END + 1
    V3_PROB = V3_SM1 + 1
    T3_TP = lambda t_: T3_SC_END + t_ + 1
    T3_TP_END = T3_SC_END + NT
    V3_TPEV = lambda t_: V3_PROB + t_ + 1
    V3_TPEV_END = V3_PROB + NT
    S3T_PROBS = S3T_H2T

    with nc.Block() as blk:
        @blk.sync
        def _(s):
            s.wait_ge(sH2S[0], 16 * (NT // 2))
            s.wait_ge(sH2S[1], 16 * (NT // 2))
            for kc in range(8):
                s.dma_start(h2T[:, kc, :], h2h[:, 128 * kc:128 * (kc + 1)], transpose=True
                            ).then_inc(sTT, 16)

        @blk.tensor
        def _(t):
            t.wait_ge(sTT, 128)
            for t_ in range(NT):
                if t_ >= 1:
                    t.wait_ge(sV, V3_SCEV(t_ - 1))
                for kc in range(8):
                    ins = t.matmul(ps_sc[:, :], h2T[:, kc, 128 * t_:128 * (t_ + 1)],
                                   gwT[:, kc, :], start=(kc == 0), stop=(kc == 7))
                ins.then_inc(sT, 1)

        @blk.vector
        def _(v):
            for t_ in range(NT):
                v.wait_ge(sT, T3_SC(t_))
                v.tensor_copy(sc_tm[:, t_, :], ps_sc[:, :]).then_inc(sV, 1)
            v.drain()
            v.tensor_reduce(smx[:, :, 0:1], sc_tm[:, :, :], axis=AX.X, op=AL.max)
            v.drain()
            v.tensor_sub(sc_tm[:, :, :], sc_tm[:, :, :],
                         smx[:, :, 0:1].broadcast_to([128, NT, 8])).then_inc(sV, 1)

        @blk.scalar
        def _(sc):
            sc.wait_ge(sV, V3_SM1)
            sc.activation(prob_tm[:].rearrange("p a b -> p (a b)"),
                          sc_tm[:].rearrange("p a b -> p (a b)"), AF.Exp).then_inc(sC, 1)

        @blk.vector
        def _(v):
            v.wait_ge(sC, C3_EXP)
            v.tensor_reduce(smx[:, :, 1:2], prob_tm[:, :, :], axis=AX.X, op=AL.add)
            v.drain()
            v.reciprocal(smx[:, :, 0:1], smx[:, :, 1:2])
            v.drain()
            v.tensor_mul(prob_tm[:, :, :], prob_tm[:, :, :],
                         smx[:, :, 0:1].broadcast_to([128, NT, 8]))
            v.drain()
            v.tensor_copy(prob_bf[:, :, :], prob_tm[:, :, :]).then_inc(sV, 1)

        @blk.tensor
        def _(t):
            t.wait_ge(sV, V3_PROB)
            for t_ in range(NT):
                if t_ >= 1:
                    t.wait_ge(sV, V3_TPEV(t_ - 1))
                t.matmul(ps_tp[:, :], prob_bf[:, t_, :], ident[:, :],
                         start=True, stop=True).then_inc(sT, 1)

        @blk.vector
        def _(v):
            for t_ in range(NT):
                v.wait_ge(sT, T3_TP(t_))
                v.tensor_copy(probsT[:, 128 * t_:128 * (t_ + 1)], ps_tp[:, :]).then_inc(sV, 1)

        @blk.sync
        def _(s):
            s.wait_ge(sV, V3_TPEV_END)
            s.dma_start(probsh, probsT[:]).then_inc(sPR, 16)
            if DEBUG:
                s.dma_start(scth, sc_tm[:]).then_inc(sPR, 16)
                s.dma_start(pth, prob_tm[:]).then_inc(sPR, 16)
                for kc in range(8):
                    s.dma_start(h2td[:, kc, :], h2T[:, kc, 0:128]).then_inc(sPR, 16)

    ph3.close()

    ph4 = ExitStack()
    E4 = ph4.enter_context
    p128 = E4(nc.sbuf_tensor("p128r", [128, 256], F32))
    p16 = E4(nc.sbuf_tensor("p16r", [16, 2048], F32))
    scrr = E4(nc.sbuf_tensor("scrr", [128, 256], F32))
    cntp = E4(nc.sbuf_tensor("cntpr", [128, 1], F32))
    bst = E4(nc.sbuf_tensor("bstr", [128, 4], F32))
    selu = E4(nc.sbuf_tensor("selur", [128, 1], U32))
    sel2u = E4(nc.sbuf_tensor("sel2ur", [128, 1], U32))
    tau16 = E4(nc.sbuf_tensor("tau16r", [16, 8], F32))
    zeros16 = E4(nc.sbuf_tensor("zeros16r", [16, 256], F32))
    maskr = E4(nc.sbuf_tensor("maskr", [16, 2048], F32))
    cumr = E4(nc.sbuf_tensor("cumr", [16, 2048], F32))
    totsr = E4(nc.sbuf_tensor("totsr", [16, 8], F32))
    totsT = E4(nc.sbuf_tensor("totsTr", [8, 16], F32))
    totsTs = E4(nc.sbuf_tensor("totsTsr", [8, 16], F32))
    offsr = E4(nc.sbuf_tensor("offsr", [16, 8], F32))
    selr = E4(nc.sbuf_tensor("selr", [16, 2048], F32))
    scr16 = E4(nc.sbuf_tensor("scr16r", [16, 256], F32))
    cand_i = E4(nc.sbuf_tensor("cand_ir", [16, 2048], I16))
    cand_g = E4(nc.sbuf_tensor("cand_gr", [16, 2048], F32))
    idx16 = E4(nc.sbuf_tensor("idx16r", [16, 640], I16))
    g16 = E4(nc.sbuf_tensor("g16r", [16, 640], F32))
    nf = E4(nc.sbuf_tensor("nfr", [1, 16], U32))
    tkp1 = E4(nc.sbuf_tensor("tkp1", [16, 2048], F32))
    ps_cnt = E4(nc.psum_tensor("ps_cntr", [128, 1], F32))

    S4_P = S0_END + 48
    V4_CMP = lambda it: V3_TPEV_END + it + 1
    V4_TAU = V3_TPEV_END + NIT + 1
    T4_MM = lambda it: T3_TP_END + it + 1
    S4_TAUH = S4_P + 16
    S4_TAU16 = S4_TAUH + 16
    V4_MASK = V4_TAU + 1
    S4_TOTS = S4_TAU16 + 16
    S4_TOTST = S4_TOTS + 16
    V4_OFFS = V4_MASK + 1
    S4_OFFH = S4_TOTST + 16
    S4_OFFS = S4_OFFH + 16
    V4_CAND = V4_OFFS + 1
    P4_SG = P2_MEMSET + 1
    P4_LIB = P4_SG + 1
    S4_IDX = S4_OFFS + 128 + (16 if DEBUG else 0)
    S4_GH = S4_IDX + 16 * NE
    S4_GCOL = S4_GH + 16 * NE
    S4_END = S4_GCOL

    with nc.Block() as blk:
        @blk.sync
        def _(s):
            s.wait_ge(sPR, 16 * (11 if DEBUG else 1))
            s.dma_start(p128[:], probsh.rearrange("e (q c) -> (e q) c", q=16)).then_inc(sS, 16)
            s.dma_start(p16[:].rearrange("p (e c) -> p e c", e=8),
                        probsh.rearrange("e (p c) -> p e c", p=16)).then_inc(sS, 16)
            s.dma_start(tkp1[:], tokp1).then_inc(sS, 16)

        @blk.vector
        def _(v):
            v.wait_ge(sS, S4_P)
            v.memset(bst[:, 0:1], 0.0)
            v.memset(bst[:, 1:2], 1.0)
            v.memset(bst[:, 2:3], 0.5)
            v.memset(zeros16[:], 0.0)
            v.memset(scrr[:, 0:2], 0.0)
            v.drain()
            v.tensor_scalar(scrr[:, :], p128[:, :], bst[:, 2:3], None, AL.is_ge)
            v.drain()
            v.reduce_sum(cntp[:, :], scrr[:, :], axis=AX.X).then_inc(sV, 1)
            for it in range(1, NIT + 1):
                v.wait_ge(sT, T4_MM(it - 1))
                v.tensor_copy(bst[:, 3:4], ps_cnt[:, :])
                v.drain()
                v.tensor_scalar(selu[:, :], bst[:, 3:4], float(KSEL), None, AL.is_ge)
                v.tensor_scalar(sel2u[:, :], bst[:, 3:4], float(KSEL), None, AL.is_lt)
                v.drain()
                v.copy_predicated(bst[:, 0:1], selu[:, :], bst[:, 2:3])
                v.drain()
                lastcp = v.copy_predicated(bst[:, 1:2], sel2u[:, :], bst[:, 2:3])
                v.drain()
                if it < NIT:
                    v.drain()
                    v.tensor_add(bst[:, 2:3], bst[:, 0:1], bst[:, 1:2])
                    v.drain()
                    v.tensor_scalar(bst[:, 2:3], bst[:, 2:3], 0.5, None, AL.mult)
                    v.drain()
                    v.tensor_scalar(scrr[:, :], p128[:, :], bst[:, 2:3], None, AL.is_ge)
                    v.drain()
                    v.reduce_sum(cntp[:, :], scrr[:, :], axis=AX.X).then_inc(sV, 1)
                else:
                    lastcp.then_inc(sV, 1)

        @blk.tensor
        def _(t):
            for it in range(NIT):
                t.wait_ge(sV, V4_CMP(it))
                t.matmul(ps_cnt[:, :], gml[:, :], cntp[:, :], start=True, stop=True
                         ).then_inc(sT, 1)

        @blk.sync
        def _(s):
            s.wait_ge(sV, V4_TAU)
            s.dma_start(tauh.rearrange("(p one) -> p one", one=1), bst[:, 0:1]).then_inc(sS, 16)
            s.wait_ge(sS, S4_TAUH)
            s.dma_start(tau16[:], tauh.rearrange("(e p) -> p e", p=16),
                        allow_slow_non_contiguous=True).then_inc(sS, 16)

        @blk.vector
        def _(v):
            v.wait_ge(sS, S4_TAU16)
            for e in range(NE):
                sl = slice(256 * e, 256 * (e + 1))
                v.tensor_scalar(maskr[:, sl], p16[:, sl], tau16[:, e:e + 1], None, AL.is_ge)
                v.drain()
                v.tensor_tensor_scan(cumr[:, sl], maskr[:, sl], zeros16[:, :], 0.0, AL.add, AL.max)
                v.drain()
                ins = v.tensor_copy(totsr[:, e:e + 1], cumr[:, 256 * e + 255:256 * e + 256])
                v.drain()
            ins.then_inc(sV, 1)

        @blk.sync
        def _(s):
            s.wait_ge(sV, V4_MASK)
            s.dma_start(totsh, totsr[:]).then_inc(sS, 16)
            s.wait_ge(sS, S4_TOTS)
            s.dma_start(totsT[:], totsh.rearrange("p e -> e p"),
                        allow_slow_non_contiguous=True).then_inc(sS, 16)

        @blk.vector
        def _(v):
            v.wait_ge(sS, S4_TOTST)
            v.memset(zeros16[:], 0.0)
            v.drain()
            v.tensor_tensor_scan(totsTs[:, :], totsT[:, :], zeros16[0:8, 0:16], 0.0, AL.add, AL.max)
            v.drain()
            v.tensor_sub(totsTs[:, :], totsTs[:, :], totsT[:, :]).then_inc(sV, 1)

        @blk.sync
        def _(s):
            s.wait_ge(sV, V4_OFFS)
            s.dma_start(offsh, totsTs[:]).then_inc(sS, 16)
            s.wait_ge(sS, S4_OFFH)
            s.dma_start(offsr[:], offsh.rearrange("e p -> p e"),
                        allow_slow_non_contiguous=True).then_inc(sS, 16)

        @blk.vector
        def _(v):
            v.wait_ge(sS, S4_OFFS)
            for e in range(NE):
                sl = slice(256 * e, 256 * (e + 1))
                v.tensor_scalar(cumr[:, sl], cumr[:, sl], offsr[:, e:e + 1], None, AL.add)
                v.drain()
                v.tensor_scalar(selr[:, sl], cumr[:, sl], float(KSEL), None, AL.is_le)
                v.drain()
                v.tensor_mul(selr[:, sl], selr[:, sl], maskr[:, sl])
                v.drain()
                v.tensor_mul(scr16[:, :], selr[:, sl], tkp1[:, sl])
                v.drain()
                v.tensor_scalar(cand_i[:, sl], scr16[:, :], 1.0, None, AL.subtract)
                v.drain()
                v.tensor_scalar(scr16[:, :], p16[:, sl], 1.0, None, AL.add)
                v.drain()
                v.tensor_mul(scr16[:, :], scr16[:, :], selr[:, sl])
                v.drain()
                ins = v.tensor_scalar(cand_g[:, sl], scr16[:, :], 1.0, None, AL.subtract)
            ins.then_inc(sV, 1)

        @blk.gpsimd
        def _(g):
            g.load_library(sg_lib)
            g.wait_ge(sV, V4_CAND)
            for e in range(NE):
                sl = slice(256 * e, 256 * (e + 1))
                g.sparse_gather(idx16[:, 80 * e:80 * (e + 1)], cand_i[:, sl],
                                num_found=nf[:, 2 * e:2 * e + 1])
                ins = g.sparse_gather(g16[:, 80 * e:80 * (e + 1)], cand_g[:, sl],
                                      num_found=nf[:, 2 * e + 1:2 * e + 2])
            ins.then_inc(sP, 1)
            g.load_library(mlp_lib)

        @blk.sync
        def _(s):
            s.wait_ge(sP, P4_SG)
            for gg in range(8):
                s.dma_start(idx32[16 * gg:16 * (gg + 1), :], idx16[:, :]).then_inc(sS, 16)
            if DEBUG:
                s.wait_ge(sS, S4_OFFS + 128)
                s.dma_start(idxh, idx32[:]).then_inc(sS, 16)
            for e in range(NE):
                s.dma_start(gh[e].rearrange("(c p) -> p c", p=16), g16[:, 80 * e:80 * e + 64],
                            allow_slow_non_contiguous=True).then_inc(sS, 16)
            s.wait_ge(sS, S4_GH)
            for e in range(NE):
                s.dma_start(gcol[:, e, :], gh[e].rearrange("(c p) -> p c", p=128),
                            allow_slow_non_contiguous=True).then_inc(sS, 16)

    ph4.close()
    if STAGE < 5:
        return _fin([st])

    ph5 = ExitStack()
    E5 = ph5.enter_context
    xin = E5(nc.sbuf_tensor("xinf", [128, 1, 2, 8, 512], BF16))
    h1sb = E5(nc.sbuf_tensor("h1sbf", [128, 2, 32, 512], BF16))
    ysb = E5(nc.sbuf_tensor("ysbf", [128, 8, 512], F32))
    w1sb = E5(nc.sbuf_tensor("w1sbf", [128, 2, 8, 512], BF16))
    w2sb = E5(nc.sbuf_tensor("w2sbf", [128, 8, 4, 512], BF16))
    ps_h1 = E5(nc.psum_tensor("ps_h1f", [128, 2, 512], F32))
    ps_y = E5(nc.psum_tensor("ps_yf", [128, 2, 512], F32))

    TE = 72 if P5LV >= 2 else 64
    T5 = lambda e: T3_TP_END + NIT + TE * e
    T5_MM1 = lambda e, fg, c, fi: T5(e) + 8 * fg + 4 * c + fi + 1
    T5_MM2 = lambda e, c, sc_: T5(e) + 64 + 4 * c + sc_ + 1
    T5_END = T5(NE)
    C5 = lambda e, g: C3_EXP + 64 * e + g + 1
    C5_CHUNK = lambda e, c: C3_EXP + 64 * e + 56 + 4 * c + 4
    V5_Y = lambda e, c, sc_: V4_CAND + 8 * e + 4 * c + sc_ + 1
    V5_E = lambda e: V4_CAND + 8 * (e + 1)
    D5_GA = lambda e: 32 * (e + 1)
    D5_SC = lambda e: 16 * (e + 1)

    g5base = [0]
    d5base = [0]

    with nc.Block() as blk:
        @blk.gpsimd
        def _(g):
            if P5NE == -1:
                return
            g.wait_ge(sS, S4_END)
            if P5NE == -2:
                return
            for e in range(NE if P5LV >= 1 else P5NE):
                if e >= 1 and P5LV >= 1:
                    g.wait_ge(sT, T5(e - 1) + 64)
                for c in range(2):
                    g.dma_gather(xin[:, 0, c], h2h,
                                 idx32[:, 80 * e + 32 * c:80 * e + 32 * (c + 1)],
                                 512, 512, 1024, elem_step=1024, transpose=True
                                 ).then_inc(sDD, 16)
                g.wait_ge(sDD, 32 * (e + 1))
                if P5LV >= 1:
                    for fg in range(8):
                        gi = 8 * e + fg
                        if gi >= 2:
                            eprev, fgprev = (gi - 2) // 8, (gi - 2) % 8
                            g.wait_ge(sT, T5(eprev) + 8 * fgprev + 8)
                        g.dma_start(w1sb[:, gi % 2],
                                    w1d[e].rearrange("(kc kp) f -> kp kc f", kp=128)
                                    [:, :, 512 * fg:512 * (fg + 1)]).then_inc(sW1[gi % 2], 16)
                if P5LV >= 2:
                    for fg in range(8):
                        if e >= 1:
                            g.wait_ge(sT, T5(e - 1) + TE)
                        g.dma_start(w2sb[:, fg],
                                    w2d[e].rearrange("(fc fp) o -> fp fc o", fp=128)
                                    [:, 4 * fg:4 * (fg + 1), :]).then_inc(sW2[fg % 4], 16)
                if P5LV >= 3:
                    g.wait_ge(sV, V5_E(e))
                    if e >= 1:
                        g.wait_ge(sDS, D5_SC(e - 1))
                    g.dma_scatter_add(scat, ysb[:], idx32[:, 80 * e:80 * e + 64],
                                      1024, 1024, 512, elem_step=512).then_inc(sDS, 16)

        if P5LV >= 1:
            @blk.tensor
            def _(t):
                for e in range(NE):
                    t.wait_ge(sDD, D5_GA(e))
                    for fg in range(8):
                        gi = 8 * e + fg
                        t.wait_ge(sW1[gi % 2], 16 * (gi // 2 + 1))
                        for c in range(2):
                            for fi in range(4):
                                fc = 4 * fg + fi
                                gg = 64 * e + 8 * fg + 4 * c + fi
                                if gg >= 2:
                                    t.wait_ge(sC, C3_EXP + gg - 1)
                                for kc in range(8):
                                    ins = t.matmul(ps_h1[:, gg % 2],
                                                   w1sb[:, gi % 2, kc, 128 * fi:128 * (fi + 1)],
                                                   xin[:, 0, c, kc, :],
                                                   start=(kc == 0), stop=(kc == 7))
                                ins.then_inc(sT, 1)
                    if P5LV >= 2:
                        for c in range(2):
                            t.wait_ge(sC, C5_CHUNK(e, c))
                            for s_ in range(4):
                                t.wait_ge(sW2[s_], 16 * (2 * (e + 1)))
                            for sc_ in range(4):
                                gy = 8 * e + 4 * c + sc_
                                if gy >= 2:
                                    t.wait_ge(sV, V4_CAND + gy - 1)
                                for fc in range(32):
                                    fg2 = fc // 4
                                    gi2 = 8 * e + fg2
                                    ins = t.matmul(ps_y[:, sc_ % 2],
                                                   h1sb[:, c, fc, 128 * sc_:128 * (sc_ + 1)],
                                                   w2sb[:, fg2, fc % 4, :],
                                                   start=(fc == 0), stop=(fc == 31))
                                ins.then_inc(sT, 1)

            @blk.scalar
            def _(sc):
                for e in range(NE):
                    for fg in range(8):
                        for c in range(2):
                            for fi in range(4):
                                fc = 4 * fg + fi
                                gg = 64 * e + 8 * fg + 4 * c + fi
                                sc.wait_ge(sT, T5_MM1(e, fg, c, fi))
                                sc.activation(h1sb[:, c, fc, :], ps_h1[:, gg % 2], AF.Gelu
                                              ).then_inc(sC, 1)

        if P5LV >= 2:
            @blk.vector
            def _(v):
                for e in range(NE):
                    for c in range(2):
                        for sc_ in range(4):
                            v.wait_ge(sT, T5_MM2(e, c, sc_))
                            if e >= 1 and c == 0 and sc_ == 0 and P5LV >= 3:
                                v.wait_ge(sDS, D5_SC(e - 1))
                            k = 4 * c + sc_
                            v.tensor_scalar(ysb[:, k, :], ps_y[:, sc_ % 2],
                                            gcol[:, e, k:k + 1], None, AL.mult).then_inc(sV, 1)

    ph5.close()
    if STAGE < 6:
        w = {0: [(sDD, max(0, 32 * P5NE))],
             1: [(sC, C3_EXP + 64 * NE)],
             2: [(sV, V4_CAND + 8 * NE)],
             3: [(sDS, 16 * NE)]}[P5LV]
        return _fin([st], waits=w)

    ph6 = ExitStack()
    E6 = ph6.enter_context
    xf = E6(nc.sbuf_tensor("xff", [128, 2, OH], F32))
    sf = E6(nc.sbuf_tensor("sff", [128, 2, OH], F32))
    of = E6(nc.sbuf_tensor("off_", [128, 2, OH], F32))

    V6 = lambda t_: V5_E(NE - 1) + t_ + 1
    SF_ = lambda t_: 32 * (t_ // 2 + 1)

    with nc.Block() as blk:
        @blk.sync
        def _(s):
            s.wait_ge(sDS, D5_SC(NE - 1))
            s.wait_ge(sS, S4_END)
            for t_ in range(NT):
                if t_ >= 2:
                    s.wait_ge(sV, V6(t_ - 2))
                s.dma_start(xf[:, t_ % 2], bass.AP(x2h.tensor, t_ * 128 * D, [[D, 128], [1, OH]])
                            ).then_inc(sF[t_ % 2], 16)
                s.dma_start(sf[:, t_ % 2], scat.rearrange("(n p) c -> n p c", p=128)[t_]
                            ).then_inc(sF[t_ % 2], 16)
                s.wait_ge(sV, V6(t_))
                s.dma_start(yout.rearrange("(n p) c -> n p c", p=128)[t_], of[:, t_ % 2]
                            ).then_inc(sY[t_ % 2], 16)
            s.wait_ge(sY[0], 16 * (NT // 2))
            s.wait_ge(sY[1], 16 * (NT // 2))

        @blk.vector
        def _(v):
            for t_ in range(NT):
                i = t_ % 2
                v.wait_ge(sF[t_ % 2], SF_(t_))
                if t_ >= 2:
                    v.wait_ge(sY[t_ % 2], 16 * (t_ // 2))
                    v.drain()
                v.tensor_mul(of[:, i], sf[:, i], gbt[:])
                v.drain()
                v.tensor_add(of[:, i], of[:, i], xf[:, i]).then_inc(sV, 1)

    ph6.close()
    st.close()
    nc.compile()
    return nc


_NC_CACHE = [None]


def _consts():
    ident = np.eye(128).astype(ml_dtypes.bfloat16)
    identf = np.eye(128).astype(np.float32)
    col = np.arange(128, dtype=np.float32)
    dtile = np.maximum(col[None, :] - col[:, None] - 1.0, 0.0).astype(np.float32)
    caus = (col[None, :] > col[:, None]).astype(np.float32)
    iotar = np.tile(col[None, :], (128, 1)).astype(np.float32)
    iota127 = (127.0 - col)[:, None].astype(np.float32)
    gmat = np.zeros((128, 128), np.float32)
    for p in range(128):
        gmat[p, (p // 16) * 16:(p // 16) * 16 + 16] = 1.0
    tokp1 = np.zeros((16, 2048), np.float32)
    for e in range(8):
        for p in range(16):
            tokp1[p, 256 * e:256 * (e + 1)] = 256 * p + np.arange(256) + 1
    return {"identc": ident, "identf": identf, "dtile": dtile,
            "caus": caus, "iotar": iotar, "iota127": iota127,
            "gmat": gmat, "tokp1": tokp1}


def _core_inputs(inputs, b, hh):
    f32 = np.float32
    x = inputs["L0_attn_out"][b].astype(f32, copy=False)
    ww = inputs["L0_attn_W_write"].astype(f32, copy=False)
    wr = inputs["L0_attn_W_read"].astype(f32, copy=False)
    dec = inputs["L0_attn_decay"].astype(f32, copy=False)
    la = inputs["L0_attn_log_alpha"].astype(f32, copy=False)
    gw = inputs["L0_ffn_gate_w"].astype(f32, copy=False)
    w1 = inputs["L0_ffn_w1"].astype(f32, copy=False)
    w2 = inputs["L0_ffn_w2"].astype(f32, copy=False)
    cv = inputs["L0_c"][b].astype(f32, copy=False)
    wm = inputs["L0_wmod"].astype(f32, copy=False)
    if hh == 1:
        perm = np.r_[512:1024, 0:512]
        hperm = np.r_[8:16, 0:8]
        x = x[:, perm]
        ww = ww[perm][:, perm]
        wr = wr[perm][:, perm]
        dec = dec[hperm]
        la = la[hperm]
        gw = gw[:, perm]
        w1 = w1[:, perm, :]
        modperm = np.concatenate([r * 1024 + perm for r in range(6)])
        wm = wm[:, modperm]
        w2 = w2[:, :, perm][:, :, 0:OH]
    else:
        w2 = w2[:, :, 0:OH]
    d = {"x": np.ascontiguousarray(x), "wwr": np.ascontiguousarray(ww),
         "wrd": np.ascontiguousarray(wr), "decay": np.ascontiguousarray(dec),
         "logal": np.ascontiguousarray(la), "gatew": np.ascontiguousarray(gw),
         "w1": np.ascontiguousarray(w1), "w2": np.ascontiguousarray(w2),
         "cvec": np.ascontiguousarray(cv), "wmod": np.ascontiguousarray(wm)}
    d.update(_consts())
    return d


def kernel(L0_attn_out, L0_attn_W_write, L0_attn_W_read, L0_attn_decay,
           L0_attn_log_alpha, L0_ffn_gate_w, L0_ffn_w1, L0_ffn_w2, L0_c, L0_wmod):
    inputs = {
        "L0_attn_out": np.asarray(L0_attn_out), "L0_attn_W_write": np.asarray(L0_attn_W_write),
        "L0_attn_W_read": np.asarray(L0_attn_W_read), "L0_attn_decay": np.asarray(L0_attn_decay),
        "L0_attn_log_alpha": np.asarray(L0_attn_log_alpha),
        "L0_ffn_gate_w": np.asarray(L0_ffn_gate_w), "L0_ffn_w1": np.asarray(L0_ffn_w1),
        "L0_ffn_w2": np.asarray(L0_ffn_w2), "L0_c": np.asarray(L0_c),
        "L0_wmod": np.asarray(L0_wmod),
    }
    if _NC_CACHE[0] is None:
        _NC_CACHE[0] = build()
    nc = _NC_CACHE[0]
    in_maps = [_core_inputs(inputs, core // 2, core % 2) for core in range(8)]
    res = run_bass_kernel_spmd(nc, in_maps, core_ids=list(range(8)))
    out = np.zeros((4, T, D), np.float32)
    for core in range(8):
        b, hh = core // 2, core % 2
        sh = res.results[core]["y"]
        if hh == 0:
            out[b][:, 0:OH] = sh
        else:
            out[b][:, OH:D] = sh
    return out


# version 06
# speedup vs baseline: 1.0729x; 1.0729x over previous
import numpy as np
import ml_dtypes
import concourse.bacc as bacc
import concourse.bass as bass
import concourse.mybir as mybir
from concourse.bass_utils import run_bass_kernel_spmd
from concourse.library_config import sparse_gather as sg_lib, mlp as mlp_lib
from contextlib import ExitStack

F32 = mybir.dt.float32
BF16 = mybir.dt.bfloat16
U32 = mybir.dt.uint32
I16 = mybir.dt.int16
AL = mybir.AluOpType
AX = mybir.AxisListType
AF = mybir.ActivationFunctionType

T = 4096
D = 1024
NH = 16
CH = 128
NCH = T // CH
NT = 32
NE = 8
KSEL = 1024
FF = 4096
OH = 512
NIT = 28

DEBUG = False
STAGE = 9
P5LV = 3
P5NE = 8


def build():
    nc = bacc.Bacc("TRN2")

    xd = nc.dram_tensor("x", [T, D], F32, kind="ExternalInput").ap()
    wwr = nc.dram_tensor("wwr", [D, D], F32, kind="ExternalInput").ap()
    wrd = nc.dram_tensor("wrd", [D, D], F32, kind="ExternalInput").ap()
    decay = nc.dram_tensor("decay", [NH], F32, kind="ExternalInput").ap()
    logal = nc.dram_tensor("logal", [NH], F32, kind="ExternalInput").ap()
    gatew = nc.dram_tensor("gatew", [NE, D], F32, kind="ExternalInput").ap()
    w1d = nc.dram_tensor("w1", [NE, D, FF], F32, kind="ExternalInput").ap()
    w2d = nc.dram_tensor("w2", [NE, FF, OH], F32, kind="ExternalInput").ap()
    cvec = nc.dram_tensor("cvec", [D], F32, kind="ExternalInput").ap()
    wmodd = nc.dram_tensor("wmod", [D, 6 * D], F32, kind="ExternalInput").ap()
    identd = nc.dram_tensor("identc", [128, 128], BF16, kind="ExternalInput").ap()
    identf = nc.dram_tensor("identf", [128, 128], F32, kind="ExternalInput").ap()
    dtile = nc.dram_tensor("dtile", [128, 128], F32, kind="ExternalInput").ap()
    caus = nc.dram_tensor("caus", [128, 128], F32, kind="ExternalInput").ap()
    iotar = nc.dram_tensor("iotar", [128, 128], F32, kind="ExternalInput").ap()
    iota127 = nc.dram_tensor("iota127", [128, 1], F32, kind="ExternalInput").ap()
    gmat = nc.dram_tensor("gmat", [128, 128], F32, kind="ExternalInput").ap()
    tokp1 = nc.dram_tensor("tokp1", [16, 2048], F32, kind="ExternalInput").ap()

    yout = nc.dram_tensor("y", [T, OH], F32, kind="ExternalOutput").ap()
    ik = "ExternalOutput" if DEBUG else "Internal"
    h1h = nc.dram_tensor("h1h", [T, D], BF16, kind=ik).ap()
    baseh = nc.dram_tensor("baseh", [T, D], F32, kind=ik).ap()
    x2h = nc.dram_tensor("x2h", [T, D], F32, kind=ik).ap()
    h2h = nc.dram_tensor("h2h", [T, D], BF16, kind=ik).ap()
    probsh = nc.dram_tensor("probsh", [NE, T], F32, kind=ik).ap()
    gh = nc.dram_tensor("gh", [NE, KSEL], F32, kind=ik).ap()
    idxh = nc.dram_tensor("idxh", [128, 640], I16, kind=ik).ap()
    scat = nc.dram_tensor("scat", [T, OH], F32, kind=ik).ap()
    tauh = nc.dram_tensor("tauh", [128], F32, kind=ik).ap()
    lgh = nc.dram_tensor("lgh", [NH], F32, kind="Internal").ap()
    alh = nc.dram_tensor("alh", [NH], F32, kind="Internal").ap()
    gch = nc.dram_tensor("gch", [NH], F32, kind="Internal").ap()
    modh = nc.dram_tensor("modh", [6 * D], F32, kind="Internal").ap()
    garh = nc.dram_tensor("garh", [D], F32, kind="Internal").ap()
    gbrh = nc.dram_tensor("gbrh", [D], F32, kind="Internal").ap()
    totsh = nc.dram_tensor("totsh", [16, 8], F32, kind="Internal").ap()
    offsh = nc.dram_tensor("offsh", [8, 16], F32, kind="Internal").ap()
    if DEBUG:
        scth = nc.dram_tensor("scth", [128, NT, 8], F32, kind="ExternalOutput").ap()
        pth = nc.dram_tensor("pth", [128, NT, 8], F32, kind="ExternalOutput").ap()
        h2td = nc.dram_tensor("h2td", [128, 8, 128], BF16, kind="ExternalOutput").ap()

    st = ExitStack()
    E = st.enter_context

    ident = E(nc.sbuf_tensor("ident", [128, 128], BF16))
    identfs = E(nc.sbuf_tensor("identfs", [128, 128], F32))
    gwT = E(nc.sbuf_tensor("gwT", [128, 8, NE], BF16))
    gbt = E(nc.sbuf_tensor("gbt", [128, OH], F32))
    iota127l = E(nc.sbuf_tensor("iota127l", [128, 1], F32))
    gml = E(nc.sbuf_tensor("gml", [128, 128], F32))
    d16 = E(nc.sbuf_tensor("d16", [16, 4], F32))
    lgb = E(nc.sbuf_tensor("lgb", [128, 16], F32))
    lgp = E(nc.sbuf_tensor("lgp", [128, 8], F32))
    c128 = E(nc.sbuf_tensor("c128", [128, 8], F32))
    silc = E(nc.sbuf_tensor("silc", [128, 8], BF16))
    mod_sb = E(nc.sbuf_tensor("mod_sb", [128, 48], F32))
    acol = E(nc.sbuf_tensor("acol", [128, 8], F32))
    zero512 = E(nc.sbuf_tensor("zero512", [128, 512], F32))
    stats = E(nc.sbuf_tensor("statsb", [128, NT, 8], F32))
    stats2 = E(nc.sbuf_tensor("stats2b", [128, NT, 8], F32))
    gcol = E(nc.sbuf_tensor("gcol", [128, NE, 8], F32))
    idx32 = E(nc.sbuf_tensor("idx32r", [128, 640], I16))
    scrD = E(nc.sbuf_tensor("scrD", [128, D], F32))
    scrD2 = E(nc.sbuf_tensor("scrD2", [128, D], F32))
    ph02 = ExitStack()
    E02 = ph02.enter_context
    wwT = E02(nc.sbuf_tensor("wwT", [128, 8, D], BF16))
    wrT = E02(nc.sbuf_tensor("wrT", [128, 8, D], BF16))
    mtile = E02(nc.sbuf_tensor("mtile", [128, 16, 128], BF16))
    gppair = E02(nc.sbuf_tensor("gppair", [128, 8, 128], BF16))
    gwcols = E02(nc.sbuf_tensor("gwcols", [128, 16], BF16))
    gcq = E02(nc.sbuf_tensor("gcq", [128, 8], F32))
    ca1t = E02(nc.sbuf_tensor("ca1t", [128, D], BF16))
    sat = E02(nc.sbuf_tensor("sat", [128, D], BF16))
    gat = E02(nc.sbuf_tensor("gat", [128, D], BF16))
    cb1t = E02(nc.sbuf_tensor("cb1t", [128, D], BF16))
    sbt = E02(nc.sbuf_tensor("sbt", [128, D], BF16))

    sS = E(nc.semaphore("sS"))
    sG = E(nc.semaphore("sG"))
    sV = E(nc.semaphore("sV"))
    sC = E(nc.semaphore("sC"))
    sT = E(nc.semaphore("sT"))
    sP = E(nc.semaphore("sP"))
    sDD = E(nc.semaphore("sDD"))
    sDS = E(nc.semaphore("sDS"))
    sWM = [E(nc.semaphore("sWM0")), E(nc.semaphore("sWM1"))]
    sX = [E(nc.semaphore("sX0")), E(nc.semaphore("sX1"))]
    sH = [E(nc.semaphore("sH0")), E(nc.semaphore("sH1"))]
    sBS = [E(nc.semaphore("sBS0")), E(nc.semaphore("sBS1"))]
    sCH = [E(nc.semaphore("sCH0")), E(nc.semaphore("sCH1"))]
    sX2S = [E(nc.semaphore("sX2S0")), E(nc.semaphore("sX2S1"))]
    sH2S = [E(nc.semaphore("sH2S0")), E(nc.semaphore("sH2S1"))]
    sTT = E(nc.semaphore("sTT"))
    sPR = E(nc.semaphore("sPR"))
    sW1 = [E(nc.semaphore(f"sW1_{i}")) for i in range(4)]
    sW2 = [E(nc.semaphore(f"sW2_{i}")) for i in range(4)]
    sF = [E(nc.semaphore("sF0")), E(nc.semaphore("sF1"))]
    sY = [E(nc.semaphore("sY0")), E(nc.semaphore("sY1"))]
    sXA = [E(nc.semaphore("sXA0")), E(nc.semaphore("sXA1"))]
    sX3 = [E(nc.semaphore("sX3_0")), E(nc.semaphore("sX3_1"))]
    sCB = [E(nc.semaphore("sCB0")), E(nc.semaphore("sCB1"))]
    sXF = E(nc.semaphore("sXF"))
    sW2F = [E(nc.semaphore("sW2F0")), E(nc.semaphore("sW2F1"))]
    sWC = E(nc.semaphore("sWC"))


    def _fin(stacks, waits=()):
        if waits:
            with nc.Block() as blk_:
                @blk_.sync
                def _(s):
                    for sem_, val_ in waits:
                        s.wait_ge(sem_, val_)
        for s_ in stacks:
            s_.close()
        nc.compile()
        return nc
    ph0 = ExitStack()
    E0 = ph0.enter_context
    wwn = E0(nc.sbuf_tensor("wwn", [128, 8, D], BF16))
    wrn = E0(nc.sbuf_tensor("wrn", [128, 8, D], BF16))
    gwn = E0(nc.sbuf_tensor("gwn", [8, D], BF16))
    wmt = E0(nc.sbuf_tensor("wmt", [128, 2, 8, 128], BF16))
    dtl = E0(nc.sbuf_tensor("dtl", [128, 128], F32))
    causl = E0(nc.sbuf_tensor("causl", [128, 128], F32))
    iotarl = E0(nc.sbuf_tensor("iotarl", [128, 128], F32))
    mrow = E0(nc.sbuf_tensor("mrow", [128, 16, 128], F32))
    rows4 = E0(nc.sbuf_tensor("rows4", [1, 4, D], F32))
    rowb = rows4[0:1, 0]
    rowb2 = rows4[0:1, 1]
    rowb3 = rows4[0:1, 2]
    prow = rows4[0:1, 3]
    pmod = E0(nc.psum_tensor("pmod", [128, 48], F32))
    ptrA = E0(nc.psum_tensor("ptrA", [128, 128], BF16))
    ptrB = E0(nc.psum_tensor("ptrB", [128, 128], BF16))

    S0_CONSTS = 16 * 10
    S0_GAM = S0_CONSTS + 32
    S0_LGB = S0_GAM + 48
    S0_AL = S0_LGB + 32
    S0_ACOL = S0_AL + 64
    S0_MODH = S0_ACOL + 16
    S0_ROWS = S0_MODH + 32
    S0_GAGB = S0_ROWS + 32
    S0_END = S0_GAGB
    G0_WR = 16 * 3
    G0_ZERO = G0_WR + 16
    G0_TILES = G0_ZERO + 16 * 7
    G0_END = G0_TILES
    C0_GAM = 1
    C0_TABLES = 2
    T0_MOD = 48
    T0_TRANS = T0_MOD + 136
    V0_MT = 1
    V0_EVAC = V0_MT + 136
    V0_MODEV = V0_EVAC + 1
    V0_CLAMP = V0_MODEV + 1
    V0_TILES = V0_CLAMP + 1
    V0_END = V0_TILES

    with nc.Block() as blk:
        @blk.sync
        def _(s):
            s.dma_start(ident[:], identd).then_inc(sS, 16)
            s.dma_start(identfs[:], identf).then_inc(sS, 16)
            s.dma_start(dtl[:], dtile).then_inc(sS, 16)
            s.dma_start(causl[:], caus).then_inc(sS, 16)
            s.dma_start(iotarl[:], iotar).then_inc(sS, 16)
            s.dma_start(iota127l[:], iota127).then_inc(sS, 16)
            s.dma_start(gml[:], gmat).then_inc(sS, 16)
            s.dma_start(d16[:, 0:1], decay.rearrange("(h one) -> h one", one=1)).then_inc(sS, 16)
            s.dma_start(c128[:], cvec.rearrange("(k p) -> p k", p=128),
                        allow_slow_non_contiguous=True).then_inc(sS, 16)
            s.dma_start(d16[:, 1:2], logal.rearrange("(h one) -> h one", one=1)).then_inc(sS, 16)

        @blk.gpsimd
        def _(g):
            g.dma_start(wwn[:], wwr.rearrange("(ic ip) k -> ip ic k", ip=128)).then_inc(sG, 16)
            g.dma_start(wrn[:], wrd.rearrange("(oc op) f -> op oc f", op=128)).then_inc(sG, 16)
            g.dma_start(gwn[:], gatew).then_inc(sG, 16)
            g.wait_ge(sG, 48)
            g.memset(zero512[:], 0.0)
            g.drain()
            g.dma_start(scat.rearrange("(t p) c -> p t c", p=128),
                        zero512[:].unsqueeze(1).broadcast_to([128, NT, 512])).then_inc(sG, 16)
            g.wait_ge(sG, 64)
            for cc in range(48):
                if cc >= 2:
                    g.wait_ge(sT, cc - 1)
                g.dma_start(wmt[:, cc % 2], wmodd.rearrange("(kc kp) f -> kp kc f", kp=128)
                            [:, :, 128 * cc:128 * (cc + 1)]).then_inc(sWM[cc % 2], 16)

        @blk.scalar
        def _(sc):
            sc.wait_ge(sS, S0_CONSTS)
            sc.activation(d16[:, 2:3], d16[:, 0:1], AF.Sigmoid)
            sc.drain()
            sc.activation(d16[:, 3:4], d16[:, 2:3], AF.Ln)
            sc.activation(silc[:], c128[:], AF.Silu).then_inc(sC, 1)
            sc.wait_ge(sS, S0_LGB)
            sc.activation(d16[:, 0:1], d16[:, 1:2], AF.Exp)
            sc.activation(acol[0:16, 0:1], d16[:, 3:4], AF.Exp, scale=float(CH))
            for h in range(16):
                sc.activation(mrow[:, h, :], dtl[:], AF.Exp, scale=lgb[:, h:h + 1])
            for hp in range(8):
                sc.activation(gppair[:, hp, :], iotarl[:], AF.Exp, scale=lgp[:, hp:hp + 1])
            for h in range(16):
                ins = sc.activation(gwcols[:, h:h + 1], iota127l[:], AF.Exp, scale=lgb[:, h:h + 1])
            ins.then_inc(sC, 1)

        @blk.sync
        def _(s):
            s.wait_ge(sC, C0_GAM)
            s.dma_start(lgh.rearrange("(h one) -> h one", one=1), d16[:, 3:4]).then_inc(sS, 16)
            s.dma_start(alh.rearrange("(h one) -> h one", one=1), d16[:, 3:4]).then_inc(sS, 16)
            s.wait_ge(sS, S0_GAM)
            s.dma_start(lgb[:], bass.AP(lgh.tensor, 0, [[0, 128], [1, 16]]),
                        allow_slow_non_contiguous=True).then_inc(sS, 16)
            for h2_ in range(2):
                s.dma_start(lgp[64 * h2_:64 * (h2_ + 1), :],
                            bass.AP(lgh.tensor, h2_, [[0, 64], [2, 8]]),
                            allow_slow_non_contiguous=True).then_inc(sS, 16)
            s.wait_ge(sC, C0_TABLES)
            s.dma_start(alh.rearrange("(h one) -> h one", one=1), d16[:, 0:1]).then_inc(sS, 16)
            s.dma_start(gch.rearrange("(h one) -> h one", one=1), acol[0:16, 0:1]).then_inc(sS, 16)
            s.wait_ge(sS, S0_AL)
            for h2_ in range(2):
                s.dma_start(acol[64 * h2_:64 * (h2_ + 1), :],
                            bass.AP(alh.tensor, h2_, [[0, 64], [2, 8]]),
                            allow_slow_non_contiguous=True).then_inc(sS, 16)
                s.dma_start(gcq[64 * h2_:64 * (h2_ + 1), :],
                            bass.AP(gch.tensor, h2_, [[0, 64], [2, 8]]),
                            allow_slow_non_contiguous=True).then_inc(sS, 16)

        @blk.vector
        def _(v):
            v.wait_ge(sC, C0_TABLES)
            for h in range(16):
                ins = v.tensor_mul(mtile[:, h, :], mrow[:, h, :], causl[:])
            ins.then_inc(sV, 1)

        @blk.tensor
        def _(t):
            t.wait_ge(sC, C0_GAM)
            for cc in range(48):
                t.wait_ge(sWM[cc % 2], 16 * (cc // 2 + 1))
                for kc in range(8):
                    ins = t.matmul(pmod[:, cc:cc + 1], wmt[:, cc % 2, kc, :], silc[:, kc:kc + 1],
                                   start=(kc == 0), stop=(kc == 7))
                ins.then_inc(sT, 1)
            ntr = 0
            for src in (wwn, wrn):
                for oc in range(8):
                    for kc in range(8):
                        if ntr >= 2:
                            t.wait_ge(sV, V0_MT + ntr - 1)
                        dst = ptrA if ntr % 2 == 0 else ptrB
                        t.transpose(dst[:, :], src[:, oc, 128 * kc:128 * (kc + 1)], ident[:]
                                    ).then_inc(sT, 1)
                        ntr += 1
            for kc in range(8):
                t.wait_ge(sV, V0_MT + ntr - 1)
                dst = ptrA if ntr % 2 == 0 else ptrB
                t.transpose(dst[0:128, 0:8], gwn[:, 128 * kc:128 * (kc + 1)], ident[0:8, 0:8]
                            ).then_inc(sT, 1)
                ntr += 1

        @blk.vector
        def _(v):
            v.wait_ge(sS, S0_ACOL)
            ntr = 0
            for which in ("ww", "wr"):
                for oc in range(8):
                    for kc in range(8):
                        v.wait_ge(sT, T0_MOD + ntr + 1)
                        src = ptrA if ntr % 2 == 0 else ptrB
                        if which == "ww":
                            v.tensor_copy(wwT[:, kc, 128 * oc:128 * (oc + 1)], src[:, :]
                                          ).then_inc(sV, 1)
                        else:
                            v.tensor_scalar(wrT[:, kc, 128 * oc:128 * (oc + 1)], src[:, :],
                                            acol[:, kc:kc + 1], None, AL.mult).then_inc(sV, 1)
                        ntr += 1
            for kc in range(8):
                v.wait_ge(sT, T0_MOD + ntr + 1)
                src = ptrA if ntr % 2 == 0 else ptrB
                v.tensor_copy(gwT[:, kc, :], src[0:128, 0:8]).then_inc(sV, 1)
                ntr += 1
            v.wait_ge(sT, T0_MOD)
            v.tensor_copy(mod_sb[:], pmod[:]).then_inc(sV, 1)

        @blk.sync
        def _(s):
            s.wait_ge(sV, V0_MODEV)
            s.dma_start(modh.rearrange("(c p) -> p c", p=128), mod_sb[:],
                        allow_slow_non_contiguous=True).then_inc(sS, 16)
            s.wait_ge(sS, S0_MODH)
            s.dma_start(rowb, modh[2 * D:3 * D].rearrange("(one d) -> one d", one=1)
                        ).then_inc(sS, 16)
            s.dma_start(rowb2, modh[5 * D:6 * D].rearrange("(one d) -> one d", one=1)
                        ).then_inc(sS, 16)

        @blk.vector
        def _(v):
            v.wait_ge(sS, S0_ROWS)
            for rb in (rowb, rowb2):
                v.drain()
                v.tensor_scalar(prow, rb, 0.0, None, AL.is_ge)
                v.drain()
                v.tensor_scalar(prow, prow, 2.0, 1.0, AL.mult, op1=AL.subtract)
                v.tensor_scalar(rowb3, rb, -1.0, None, AL.mult)
                v.drain()
                v.tensor_max(rowb3, rowb3, rb)
                v.drain()
                v.tensor_scalar_max(rowb3, rowb3, 0.25)
                v.drain()
                ins = v.tensor_mul(rb, prow, rowb3)
            ins.then_inc(sV, 1)

        @blk.sync
        def _(s):
            s.wait_ge(sV, V0_CLAMP)
            s.dma_start(garh.rearrange("(one d) -> one d", one=1), rowb).then_inc(sS, 16)
            s.dma_start(gbrh.rearrange("(one d) -> one d", one=1), rowb2).then_inc(sS, 16)

        @blk.gpsimd
        def _(g):
            g.wait_ge(sS, S0_GAGB)
            g.dma_start(sat[:], bass.AP(modh.tensor, 0 * D, [[0, 128], [1, D]]),
                        allow_slow_non_contiguous=True).then_inc(sG, 16)
            g.dma_start(ca1t[:], bass.AP(modh.tensor, 1 * D, [[0, 128], [1, D]]),
                        allow_slow_non_contiguous=True).then_inc(sG, 16)
            g.dma_start(sbt[:], bass.AP(modh.tensor, 3 * D, [[0, 128], [1, D]]),
                        allow_slow_non_contiguous=True).then_inc(sG, 16)
            g.dma_start(cb1t[:], bass.AP(modh.tensor, 4 * D, [[0, 128], [1, D]]),
                        allow_slow_non_contiguous=True).then_inc(sG, 16)
            g.dma_start(gat[:], bass.AP(garh.tensor, 0, [[0, 128], [1, D]]),
                        allow_slow_non_contiguous=True).then_inc(sG, 16)
            g.dma_start(gbt[:], bass.AP(gbrh.tensor, 0, [[0, 128], [1, OH]]),
                        allow_slow_non_contiguous=True).then_inc(sG, 16)
            g.dma_start(scrD[0:1, :], bass.AP(garh.tensor, 0, [[0, 1], [1, D]]),
                        allow_slow_non_contiguous=True).then_inc(sG, 16)

        @blk.vector
        def _(v):
            v.wait_ge(sG, G0_TILES)
            v.tensor_scalar(ca1t[:], ca1t[:], 1.0, None, AL.add)
            v.tensor_scalar(cb1t[:], cb1t[:], 1.0, None, AL.add)
            for kc in range(8):
                ins = v.tensor_mul(wrT[:, kc, :], wrT[:, kc, :], gat[:, :])
            ins.then_inc(sV, 1)

    ph0.close()
    if STAGE < 1:
        return _fin([ph02, st])

    ph12 = ExitStack()
    E1 = ph12.enter_context
    h1T = E1(nc.sbuf_tensor("h1T", [128, 8, 64 + T], BF16))
    xt = E1(nc.sbuf_tensor("xt", [128, 2, D], F32))
    ut = E1(nc.sbuf_tensor("ut", [128, 2, D], BF16))
    h1t_ = E1(nc.sbuf_tensor("h1t_", [128, 2, D], BF16))
    tmpt = E1(nc.sbuf_tensor("tmpt", [128, 2, D], BF16))
    baset = E1(nc.sbuf_tensor("baset", [128, 2, D], F32))
    ph1ps = ExitStack()
    pt1 = ph1ps.enter_context(nc.psum_tensor("pt1", [128, 128], BF16))
    pt2 = ph1ps.enter_context(nc.psum_tensor("pt2", [128, 128], BF16))

    SX_A = lambda t_: 16 * (t_ // 2 + 1)
    SX_B = lambda t_: 16 * (t_ // 2 + 1)
    SX_3 = lambda t_: 16 * (t_ // 2 + 1)
    V1_ST = lambda t_: V0_END + (t_ + 1)
    V1_CHAIN = V0_END + NT + 1
    C1_SQRT = C0_TABLES + 1
    V1_RSTD = V1_CHAIN + 1
    SH_ = lambda t_: 16 * (t_ // 2 + 1)
    S1_END = S0_END
    C1_U = lambda t_: C1_SQRT + t_ + 1
    C1_END = C1_SQRT + NT
    _v1_h1, _v1_base, _v1_ev = {}, {}, {}
    _tok = 0
    for _t in range(NT):
        _tok += 1; _v1_h1[_t] = _tok
        _tok += 1; _v1_base[_t] = _tok
        if _t >= 1:
            for _kc in range(8):
                _tok += 1; _v1_ev[(_t - 1, _kc)] = _tok
    for _kc in range(8):
        _tok += 1; _v1_ev[(NT - 1, _kc)] = _tok
    _tok += 1
    _V1_TOTAL = _tok
    V1_H1 = lambda t_: V1_RSTD + _v1_h1[t_]
    V1_BASE = lambda t_: V1_RSTD + _v1_base[t_]
    V1_EV = lambda t_, kc: V1_RSTD + _v1_ev[(t_, kc)]
    T1_TR = lambda t_, kc: T0_TRANS + 8 * t_ + kc + 1
    T1_END = T0_TRANS + 8 * NT
    V1_PAD = V1_RSTD + _V1_TOTAL
    V1_END = V1_PAD

    with nc.Block() as blk:
        @blk.scalar
        def _(sc):
            for t_ in range(NT):
                if t_ >= 2:
                    sc.wait_ge(sV, V1_ST(t_ - 2))
                sc.dma_start(xt[:, t_ % 2], xd.rearrange("(n p) d -> n p d", p=128)[t_]
                             ).then_inc(sXA[t_ % 2], 16)

        @blk.sync
        def _(s):
            for t_ in range(NT):
                if t_ >= 2:
                    s.wait_ge(sV, V1_BASE(t_ - 2))
                elif t_ == 0:
                    s.wait_ge(sV, V1_ST(NT - 2))
                else:
                    s.wait_ge(sV, V1_ST(NT - 1))
                s.dma_start(xt[:, t_ % 2], xd.rearrange("(n p) d -> n p d", p=128)[t_]
                            ).then_inc(sX[t_ % 2], 16)

        @blk.gpsimd
        def _(g):
            for t_ in range(NT):
                g.wait_ge(sV, V1_H1(t_))
                g.dma_start(h1h.rearrange("(n p) d -> n p d", p=128)[t_], h1t_[:, t_ % 2]
                            ).then_inc(sH[t_ % 2], 16)
                g.wait_ge(sV, V1_BASE(t_))
                g.dma_start(baseh.rearrange("(n p) d -> n p d", p=128)[t_], baset[:, t_ % 2]
                            ).then_inc(sBS[t_ % 2], 16)

        @blk.vector
        def _(v):
            for t_ in range(NT):
                v.wait_ge(sXA[t_ % 2], SX_A(t_))
                v.reduce_sum(stats[:, t_, 0:1], xt[:, t_ % 2], axis=AX.X)
                v.tensor_tensor_reduce(scrD[:, :] if t_ % 2 == 0 else scrD2[:, :],
                                       xt[:, t_ % 2], xt[:, t_ % 2], 1.0, 0.0,
                                       AL.mult, AL.add, accum_out=stats[:, t_, 1:2]
                                       ).then_inc(sV, 1)
            v.drain()
            v.tensor_scalar(stats[:, :, 2:3], stats[:, :, 0:1], 1.0 / D, None, AL.mult)
            v.drain()
            v.tensor_scalar(stats[:, :, 3:4], stats[:, :, 1:2], 1.0 / D, None, AL.mult)
            v.tensor_mul(stats[:, :, 4:5], stats[:, :, 2:3], stats[:, :, 2:3])
            v.drain()
            v.tensor_sub(stats[:, :, 3:4], stats[:, :, 3:4], stats[:, :, 4:5])
            v.drain()
            v.tensor_scalar(stats[:, :, 5:6], stats[:, :, 3:4], 1e-5, None, AL.add
                            ).then_inc(sV, 1)

        @blk.scalar
        def _(sc):
            sc.wait_ge(sV, V1_CHAIN)
            sc.activation(stats[:, :, 6:7], stats[:, :, 5:6], AF.Sqrt).then_inc(sC, 1)

        @blk.vector
        def _(v):
            v.wait_ge(sC, C1_SQRT)
            v.reciprocal(stats[:, :, 7:8], stats[:, :, 6:7])
            v.drain()
            v.tensor_mul(stats[:, :, 4:5], stats[:, :, 2:3], stats[:, :, 7:8])
            v.drain()
            v.tensor_scalar(stats[:, :, 4:5], stats[:, :, 4:5], -1.0, None, AL.mult
                            ).then_inc(sV, 1)

        @blk.scalar
        def _(sc):
            sc.wait_ge(sV, V1_RSTD)
            for t_ in range(NT):
                sc.wait_ge(sX[t_ % 2], SX_B(t_))
                if t_ >= 2:
                    sc.wait_ge(sV, V1_H1(t_ - 2))
                sc.activation(ut[:, t_ % 2], xt[:, t_ % 2], AF.Identity,
                              bias=stats[:, t_, 4:5], scale=stats[:, t_, 7:8]
                              ).then_inc(sC, 1)

        @blk.vector
        def _(v):
            def evacs(tt):
                for kc in range(8):
                    j = 8 * tt + kc
                    v.wait_ge(sT, T1_TR(tt, kc))
                    src = pt1 if j % 2 == 0 else pt2
                    v.tensor_copy(h1T[:, kc, 64 + 128 * tt:64 + 128 * (tt + 1)], src[:, :]
                                  ).then_inc(sV, 1)
            for t_ in range(NT):
                i = t_ % 2
                v.wait_ge(sC, C1_U(t_))
                if t_ >= 2:
                    v.wait_ge(sT, T1_TR(t_ - 2, 7))
                    v.wait_ge(sH[t_ % 2], SH_(t_ - 2))
                v.tensor_mul(h1t_[:, i], ut[:, i], ca1t[:])
                v.drain()
                v.tensor_add(h1t_[:, i], h1t_[:, i], sat[:]).then_inc(sV, 1)
                v.drain()
                v.tensor_mul(tmpt[:, i], h1t_[:, i], gat[:])
                v.drain()
                if t_ >= 2:
                    v.wait_ge(sBS[t_ % 2], 16 * (t_ // 2))
                sd = scrD if i == 0 else scrD2
                v.tensor_scalar(sd[:, :], tmpt[:, i], 1.0, None, AL.mult)
                v.drain()
                v.tensor_add(baset[:, i], sd[:, :], xt[:, i]).then_inc(sV, 1)
                if t_ >= 1:
                    evacs(t_ - 1)
            evacs(NT - 1)
            v.memset(h1T[:, :, 0:64], 0.0).then_inc(sV, 1)

        @blk.tensor
        def _(t):
            for t_ in range(NT):
                t.wait_ge(sV, V1_H1(t_))
                for kc in range(8):
                    j = 8 * t_ + kc
                    if j >= 2:
                        tt2, kc2 = (j - 2) // 8, (j - 2) % 8
                        t.wait_ge(sV, V1_EV(tt2, kc2))
                    dst = pt1 if j % 2 == 0 else pt2
                    t.transpose(dst[:, :], h1t_[:, t_ % 2, 128 * kc:128 * (kc + 1)], ident[:]
                                ).then_inc(sT, 1)

    ph1ps.close()
    if STAGE < 2:
        return _fin([ph12, ph02, st])

    wkb = E1(nc.sbuf_tensor("wkb", [128, 2, D], BF16))
    vcb = E1(nc.sbuf_tensor("vcb", [128, 2, D], BF16))
    vgwb = E1(nc.sbuf_tensor("vgwb", [128, 2, D], BF16))
    rkpb = E1(nc.sbuf_tensor("rkpb", [128, 8, 128], BF16))
    ssb = E1(nc.sbuf_tensor("ssb", [128, 8, 128], BF16))
    rdsb = E1(nc.sbuf_tensor("rdsb", [128, 2, D], BF16))
    stf = E1(nc.sbuf_tensor("stf", [128, 512], F32))
    stb = E1(nc.sbuf_tensor("stb", [128, 512], BF16))
    x2t = E1(nc.sbuf_tensor("x2t", [128, 2, D], F32))
    ps_v = E1(nc.psum_tensor("ps_v", [128, 1024], F32))
    ps_s = E1(nc.psum_tensor("ps_s", [128, 1024], F32))
    ps_r = E1(nc.psum_tensor("ps_r", [128, 1024], F32))
    ps_u = E1(nc.psum_tensor("ps_u", [128, 512], F32))
    ps_p = E1(nc.psum_tensor("ps_p", [128, 512], F32))

    VD = 10
    TD = 8
    V2_0 = V1_END + 1
    V2 = lambda c, k: V2_0 + VD * c + k
    T2 = lambda c, k: T1_END + TD * c + k
    SCH_ = lambda c: 16 * (c // 2 + 1)
    SCB_ = lambda c: 16 * (c // 2 + 1)
    S2_END = S1_END
    P2_MEMSET = 1
    V2_END = V2_0 + VD * NCH
    T2_END = T1_END + TD * NCH

    with nc.Block() as blk:
        @blk.gpsimd
        def _(g):
            g.wait_ge(sH[0], 16 * (NT // 2))
            g.wait_ge(sH[1], 16 * (NT // 2))
            g.memset(wkb[0:1, 0, :], 0.0).then_inc(sP, 1)

        @blk.vector
        def _(v):
            v.memset(stf[:], 0.0)
            v.memset(stb[:], 0.0).then_inc(sV, 1)

        @blk.scalar
        def _(sc):
            sc.wait_ge(sBS[0], 16 * (NT // 2))
            sc.wait_ge(sBS[1], 16 * (NT // 2))
            for c in range(NCH):
                if c >= 2:
                    sc.wait_ge(sV, V2(c - 2, 10))
                sc.dma_start(baset[:, c % 2], baseh.rearrange("(n p) d -> n p d", p=128)[c]
                             ).then_inc(sCB[c % 2], 16)

        @blk.sync
        def _(s):
            s.wait_ge(sH[0], 16 * (NT // 2))
            s.wait_ge(sH[1], 16 * (NT // 2))
            for c in range(NCH):
                if c >= 2:
                    s.wait_ge(sV, V2(c - 2, 10))
                if c == 0:
                    s.dma_start(wkb[1:128, 0, :], bass.AP(h1h.tensor, 0, [[D, 127], [1, D]])
                                ).then_inc(sCH[0], 16)
                else:
                    s.dma_start(wkb[:, c % 2, :],
                                bass.AP(h1h.tensor, (c * CH - 1) * D, [[D, 128], [1, D]])
                                ).then_inc(sCH[c % 2], 16)
                s.wait_ge(sV, V2(c, 10))
                s.dma_start(x2h.rearrange("(n p) d -> n p d", p=128)[c], x2t[:, c % 2]
                            ).then_inc(sX2S[c % 2], 16)

        @blk.tensor
        def _(t):
            t.wait_ge(sV, V1_END)
            t.wait_ge(sP, P2_MEMSET)
            for c in range(NCH):
                i = c % 2
                if c >= 1:
                    t.wait_ge(sV, V2(c - 1, 1))
                for half in range(2):
                    for kc in range(8):
                        ins = t.matmul(ps_v[:, 512 * half:512 * (half + 1)],
                                       h1T[:, kc, 64 + CH * c:64 + CH * (c + 1)],
                                       wwT[:, kc, 512 * half:512 * (half + 1)],
                                       start=(kc == 0), stop=(kc == 7))
                ins.then_inc(sT, 1)
                if c >= 1:
                    t.wait_ge(sV, V2(c - 1, 5))
                for hh_ in (0, 2, 4, 6, -1, 1, 3, 5, 7):
                    if hh_ < 0:
                        t.drain()
                        continue
                    h = hh_
                    par, hp = h % 2, h // 2
                    sl = slice(64 * par, 64 * (par + 1))
                    ins = t.matmul(ps_s[:, 128 * hh_:128 * (hh_ + 1)],
                                   h1T[sl, hp, 64 + CH * c - 1:64 + CH * (c + 1) - 1],
                                   h1T[sl, hp, 64 + CH * c:64 + CH * (c + 1)],
                                   start=True, stop=True)
                ins.then_inc(sT, 1)
                t.wait_ge(sV, V2(c, 4))
                if c >= 1:
                    t.wait_ge(sV, V2(c - 1, 8))
                for hh_ in range(8):
                    h = hh_
                    par, q = h % 2, h // 2
                    t.matmul(ps_r[64 * par:64 * (par + 1), 128 * q:128 * (q + 1)],
                             vcb[:, i, 64 * h:64 * (h + 1)], ssb[:, hh_, :],
                             start=True, stop=False, tile_position=(0, 64 * par))
                    ins = t.matmul(ps_r[64 * par:64 * (par + 1), 128 * q:128 * (q + 1)],
                                   stb[64 * par:64 * (par + 1), 64 * q:64 * (q + 1)],
                                   rkpb[64 * par:64 * (par + 1), q, :],
                                   start=False, stop=True,
                                   tile_position=(64 * par, 64 * par))
                ins.then_inc(sT, 1)
                for hh_ in (0, 2, 4, 6, -1, 1, 3, 5, 7):
                    if hh_ < 0:
                        t.drain()
                        continue
                    h = 8 + hh_
                    par, hp = h % 2, h // 2
                    sl = slice(64 * par, 64 * (par + 1))
                    ins = t.matmul(ps_s[:, 128 * hh_:128 * (hh_ + 1)],
                                   h1T[sl, hp, 64 + CH * c - 1:64 + CH * (c + 1) - 1],
                                   h1T[sl, hp, 64 + CH * c:64 + CH * (c + 1)],
                                   start=True, stop=True)
                ins.then_inc(sT, 1)
                t.wait_ge(sV, V2(c, 5))
                for hh_ in range(8):
                    h = 8 + hh_
                    par, q = h % 2, h // 2
                    t.matmul(ps_r[64 * par:64 * (par + 1), 128 * q:128 * (q + 1)],
                             vcb[:, i, 64 * h:64 * (h + 1)], ssb[:, hh_, :],
                             start=True, stop=False, tile_position=(0, 64 * par))
                    ins = t.matmul(ps_r[64 * par:64 * (par + 1), 128 * q:128 * (q + 1)],
                                   stb[64 * par:64 * (par + 1), 64 * q:64 * (q + 1)],
                                   rkpb[64 * par:64 * (par + 1), q, :],
                                   start=False, stop=True,
                                   tile_position=(64 * par, 64 * par))
                ins.then_inc(sT, 1)
                t.wait_ge(sCH[c % 2], SCH_(c))
                if c >= 1:
                    t.wait_ge(sV, V2(c - 1, 7))
                for h in range(16):
                    par, q = h % 2, h // 2
                    ins = t.matmul(ps_u[64 * par:64 * (par + 1), 64 * q:64 * (q + 1)],
                                   wkb[:, i, 64 * h:64 * (h + 1)],
                                   vgwb[:, i, 64 * h:64 * (h + 1)],
                                   start=True, stop=True, tile_position=(0, 64 * par))
                ins.then_inc(sT, 1)
                t.wait_ge(sV, V2(c, 8))
                if c >= 1:
                    t.wait_ge(sV, V2(c - 1, 10))
                for q in range(8):
                    ins = t.matmul(ps_p[:, :], rdsb[:, i, 128 * q:128 * (q + 1)],
                                   wrT[:, q, 0:512], start=(q == 0), stop=(q == 7))
                ins.then_inc(sT, 1)
                t.wait_ge(sV, V2(c, 9))
                for q in range(8):
                    ins = t.matmul(ps_p[:, :], rdsb[:, i, 128 * q:128 * (q + 1)],
                                   wrT[:, q, 512:1024], start=(q == 0), stop=(q == 7))
                ins.then_inc(sT, 1)

        @blk.vector
        def _(v):
            for c in range(NCH):
                i = c % 2
                v.wait_ge(sT, T2(c, 1))
                v.tensor_copy(vcb[:, i, 0:512], ps_v[:, 0:512])
                v.tensor_copy(vcb[:, i, 512:1024], ps_v[:, 512:1024]).then_inc(sV, 1)
                v.drain()
                v.tensor_mul(vgwb[:, i].rearrange("p (h d) -> p h d", h=16),
                             vcb[:, i].rearrange("p (h d) -> p h d", h=16),
                             gwcols[:, :].unsqueeze(2).broadcast_to([128, 16, 64])
                             ).then_inc(sV, 1)
                for hp in range(8):
                    ins = v.tensor_mul(rkpb[:, hp, :],
                                       h1T[:, hp, 64 + CH * c:64 + CH * (c + 1)],
                                       gppair[:, hp, :])
                ins.then_inc(sV, 1)
                v.wait_ge(sT, T2(c, 2))
                for hh_ in range(8):
                    ins = v.scalar_tensor_tensor(ssb[:, hh_, :],
                                                 ps_s[:, 128 * hh_:128 * (hh_ + 1)],
                                                 1.0, mtile[:, hh_, :], AL.mult, AL.mult)
                ins.then_inc(sV, 1)
                v.wait_ge(sT, T2(c, 4))
                for hh_ in range(8):
                    ins = v.scalar_tensor_tensor(ssb[:, hh_, :],
                                                 ps_s[:, 128 * hh_:128 * (hh_ + 1)],
                                                 1.0, mtile[:, 8 + hh_, :], AL.mult, AL.mult)
                ins.then_inc(sV, 1)
                v.wait_ge(sT, T2(c, 6))
                v.tensor_mul(stf[:].rearrange("p (q d) -> p q d", q=8),
                             stf[:].rearrange("p (q d) -> p q d", q=8),
                             gcq[:, :].unsqueeze(2).broadcast_to([128, 8, 64]))
                v.drain()
                v.tensor_add(stf[:], stf[:], ps_u[:]).then_inc(sV, 1)
                v.drain()
                v.tensor_copy(stb[:], stf[:]).then_inc(sV, 1)
                v.wait_ge(sT, T2(c, 5))
                v.tensor_copy(rdsb[:, i, 0:512], ps_r[:, 0:512])
                v.tensor_copy(rdsb[:, i, 512:1024], ps_r[:, 512:1024]).then_inc(sV, 1)
                v.wait_ge(sT, T2(c, 7))
                v.wait_ge(sCB[c % 2], SCB_(c))
                if c >= 2:
                    v.wait_ge(sX2S[c % 2], 16 * (c // 2))
                v.tensor_add(x2t[:, i, 0:512], baset[:, i, 0:512], ps_p[:, :]
                             ).then_inc(sV, 1)
                v.wait_ge(sT, T2(c, 8))
                v.tensor_add(x2t[:, i, 512:1024], baset[:, i, 512:1024], ps_p[:, :])
                v.drain()
                v.reduce_sum(stats2[:, c, 0:1], x2t[:, i], axis=AX.X)
                v.tensor_tensor_reduce(scrD[:, :] if i == 0 else scrD2[:, :],
                                       x2t[:, i], x2t[:, i], 1.0, 0.0,
                                       AL.mult, AL.add, accum_out=stats2[:, c, 1:2]
                                       ).then_inc(sV, 1)

    if STAGE < 3:
        return _fin([ph12, ph02, st])

    V3_CHAIN = V2_END + 1
    C3_SQRT = C1_END + 1
    V3_RSTD = V3_CHAIN + 1
    S3_END = S2_END
    C3_U = lambda t_: C3_SQRT + t_ + 1
    C3_END = C3_SQRT + NT
    V3_H2 = lambda t_: V3_RSTD + t_ + 1
    V3_H2_END = V3_RSTD + NT

    with nc.Block() as blk:
        @blk.vector
        def _(v):
            v.drain()
            v.tensor_scalar(stats2[:, :, 2:3], stats2[:, :, 0:1], 1.0 / D, None, AL.mult)
            v.drain()
            v.tensor_scalar(stats2[:, :, 3:4], stats2[:, :, 1:2], 1.0 / D, None, AL.mult)
            v.tensor_mul(stats2[:, :, 4:5], stats2[:, :, 2:3], stats2[:, :, 2:3])
            v.drain()
            v.tensor_sub(stats2[:, :, 3:4], stats2[:, :, 3:4], stats2[:, :, 4:5])
            v.drain()
            v.tensor_scalar(stats2[:, :, 5:6], stats2[:, :, 3:4], 1e-5, None, AL.add
                            ).then_inc(sV, 1)

        @blk.scalar
        def _(sc):
            sc.wait_ge(sV, V3_CHAIN)
            sc.activation(stats2[:, :, 6:7], stats2[:, :, 5:6], AF.Sqrt).then_inc(sC, 1)

        @blk.vector
        def _(v):
            v.wait_ge(sC, C3_SQRT)
            v.reciprocal(stats2[:, :, 7:8], stats2[:, :, 6:7])
            v.drain()
            v.tensor_mul(stats2[:, :, 4:5], stats2[:, :, 2:3], stats2[:, :, 7:8])
            v.drain()
            v.tensor_scalar(stats2[:, :, 4:5], stats2[:, :, 4:5], -1.0, None, AL.mult
                            ).then_inc(sV, 1)

        @blk.sync
        def _(s):
            for t_ in range(NT):
                s.wait_ge(sV, V3_H2(t_))
                s.dma_start(h2h.rearrange("(n p) d -> n p d", p=128)[t_], h1t_[:, t_ % 2]
                            ).then_inc(sH2S[t_ % 2], 16)

        @blk.scalar
        def _(sc):
            sc.wait_ge(sV, V3_RSTD)
            sc.wait_ge(sX2S[0], 16 * (NCH // 2))
            sc.wait_ge(sX2S[1], 16 * (NCH // 2))
            x2r = x2h.rearrange("(n p) d -> n p d", p=128)
            sc.dma_start(xt[:, 0], x2r[0]).then_inc(sX3[0], 16)
            sc.dma_start(xt[:, 1], x2r[1]).then_inc(sX3[1], 16)
            for t_ in range(NT):
                sc.wait_ge(sX3[t_ % 2], SX_3(t_))
                if t_ >= 2:
                    sc.wait_ge(sV, V3_H2(t_ - 2))
                sc.activation(ut[:, t_ % 2], xt[:, t_ % 2], AF.Identity,
                              bias=stats2[:, t_, 4:5], scale=stats2[:, t_, 7:8]
                              ).then_inc(sC, 1)
                if t_ + 2 < NT:
                    sc.dma_start(xt[:, t_ % 2], x2r[t_ + 2]).then_inc(sX3[t_ % 2], 16)

        @blk.vector
        def _(v):
            for t_ in range(NT):
                i = t_ % 2
                v.wait_ge(sC, C3_U(t_))
                if t_ >= 2:
                    v.wait_ge(sH2S[t_ % 2], 16 * (t_ // 2))
                v.tensor_mul(h1t_[:, i], ut[:, i], cb1t[:])
                v.drain()
                v.tensor_add(h1t_[:, i], h1t_[:, i], sbt[:]).then_inc(sV, 1)

    ph12.close()
    ph02.close()
    if STAGE < 4:
        return _fin([st])

    ph3 = ExitStack()
    E3 = ph3.enter_context
    h2T = E3(nc.sbuf_tensor("h2T", [128, 8, T], BF16))
    sc_tm = E3(nc.sbuf_tensor("sc_tm", [128, NT, 8], F32))
    prob_tm = E3(nc.sbuf_tensor("prob_tm", [128, NT, 8], F32))
    prob_bf = E3(nc.sbuf_tensor("prob_bf", [128, NT, 8], BF16))
    smx = E3(nc.sbuf_tensor("smx", [128, NT, 2], F32))
    probsT = E3(nc.sbuf_tensor("probsT", [8, T], F32))
    ps_sc = E3(nc.psum_tensor("ps_sc", [128, 8], F32))
    ps_tp = E3(nc.psum_tensor("ps_tp", [8, 128], F32))

    S3T_H2T = S3_END
    T3_SC = lambda t_: T2_END + t_ + 1
    T3_SC_END = T2_END + NT
    V3_SCEV = lambda t_: V3_H2_END + t_ + 1
    V3_SM1 = V3_H2_END + NT + 1
    C3_EXP = C3_END + 1
    V3_PROB = V3_SM1 + 1
    T3_TP = lambda t_: T3_SC_END + t_ + 1
    T3_TP_END = T3_SC_END + NT
    V3_TPEV = lambda t_: V3_PROB + t_ + 1
    V3_TPEV_END = V3_PROB + NT
    S3T_PROBS = S3T_H2T

    with nc.Block() as blk:
        @blk.sync
        def _(s):
            s.wait_ge(sH2S[0], 16 * (NT // 2))
            s.wait_ge(sH2S[1], 16 * (NT // 2))
            for kc in range(8):
                s.dma_start(h2T[:, kc, :], h2h[:, 128 * kc:128 * (kc + 1)], transpose=True
                            ).then_inc(sTT, 16)

        @blk.tensor
        def _(t):
            t.wait_ge(sTT, 128)
            for t_ in range(NT):
                if t_ >= 1:
                    t.wait_ge(sV, V3_SCEV(t_ - 1))
                for kc in range(8):
                    ins = t.matmul(ps_sc[:, :], h2T[:, kc, 128 * t_:128 * (t_ + 1)],
                                   gwT[:, kc, :], start=(kc == 0), stop=(kc == 7))
                ins.then_inc(sT, 1)

        @blk.vector
        def _(v):
            for t_ in range(NT):
                v.wait_ge(sT, T3_SC(t_))
                v.tensor_copy(sc_tm[:, t_, :], ps_sc[:, :]).then_inc(sV, 1)
            v.drain()
            v.tensor_reduce(smx[:, :, 0:1], sc_tm[:, :, :], axis=AX.X, op=AL.max)
            v.drain()
            v.tensor_sub(sc_tm[:, :, :], sc_tm[:, :, :],
                         smx[:, :, 0:1].broadcast_to([128, NT, 8])).then_inc(sV, 1)

        @blk.scalar
        def _(sc):
            sc.wait_ge(sV, V3_SM1)
            sc.activation(prob_tm[:].rearrange("p a b -> p (a b)"),
                          sc_tm[:].rearrange("p a b -> p (a b)"), AF.Exp).then_inc(sC, 1)

        @blk.vector
        def _(v):
            v.wait_ge(sC, C3_EXP)
            v.tensor_reduce(smx[:, :, 1:2], prob_tm[:, :, :], axis=AX.X, op=AL.add)
            v.drain()
            v.reciprocal(smx[:, :, 0:1], smx[:, :, 1:2])
            v.drain()
            v.tensor_mul(prob_tm[:, :, :], prob_tm[:, :, :],
                         smx[:, :, 0:1].broadcast_to([128, NT, 8]))
            v.drain()
            v.tensor_copy(prob_bf[:, :, :], prob_tm[:, :, :]).then_inc(sV, 1)

        @blk.tensor
        def _(t):
            t.wait_ge(sV, V3_PROB)
            for t_ in range(NT):
                if t_ >= 1:
                    t.wait_ge(sV, V3_TPEV(t_ - 1))
                t.matmul(ps_tp[:, :], prob_bf[:, t_, :], ident[:, :],
                         start=True, stop=True).then_inc(sT, 1)

        @blk.vector
        def _(v):
            for t_ in range(NT):
                v.wait_ge(sT, T3_TP(t_))
                v.tensor_copy(probsT[:, 128 * t_:128 * (t_ + 1)], ps_tp[:, :]).then_inc(sV, 1)

        @blk.sync
        def _(s):
            s.wait_ge(sV, V3_TPEV_END)
            s.dma_start(probsh, probsT[:]).then_inc(sPR, 16)
            if DEBUG:
                s.dma_start(scth, sc_tm[:]).then_inc(sPR, 16)
                s.dma_start(pth, prob_tm[:]).then_inc(sPR, 16)
                for kc in range(8):
                    s.dma_start(h2td[:, kc, :], h2T[:, kc, 0:128]).then_inc(sPR, 16)

    ph3.close()

    ph4 = ExitStack()
    E4 = ph4.enter_context
    p128 = E4(nc.sbuf_tensor("p128r", [128, 256], F32))
    p16 = E4(nc.sbuf_tensor("p16r", [16, 2048], F32))
    scrr = E4(nc.sbuf_tensor("scrr", [128, 256], F32))
    cntp = E4(nc.sbuf_tensor("cntpr", [128, 1], F32))
    bst = E4(nc.sbuf_tensor("bstr", [128, 4], F32))
    selu = E4(nc.sbuf_tensor("selur", [128, 1], U32))
    sel2u = E4(nc.sbuf_tensor("sel2ur", [128, 1], U32))
    tau16 = E4(nc.sbuf_tensor("tau16r", [16, 8], F32))
    zeros16 = E4(nc.sbuf_tensor("zeros16r", [16, 256], F32))
    maskr = E4(nc.sbuf_tensor("maskr", [16, 2048], F32))
    cumr = E4(nc.sbuf_tensor("cumr", [16, 2048], F32))
    totsr = E4(nc.sbuf_tensor("totsr", [16, 8], F32))
    totsT = E4(nc.sbuf_tensor("totsTr", [8, 16], F32))
    totsTs = E4(nc.sbuf_tensor("totsTsr", [8, 16], F32))
    offsr = E4(nc.sbuf_tensor("offsr", [16, 8], F32))
    selr = E4(nc.sbuf_tensor("selr", [16, 2048], F32))
    scr16 = E4(nc.sbuf_tensor("scr16r", [16, 256], F32))
    cand_i = E4(nc.sbuf_tensor("cand_ir", [16, 2048], I16))
    cand_g = E4(nc.sbuf_tensor("cand_gr", [16, 2048], F32))
    idx16 = E4(nc.sbuf_tensor("idx16r", [16, 640], I16))
    g16 = E4(nc.sbuf_tensor("g16r", [16, 640], F32))
    nf = E4(nc.sbuf_tensor("nfr", [1, 16], U32))
    tkp1 = E4(nc.sbuf_tensor("tkp1", [16, 2048], F32))
    ps_cnt = E4(nc.psum_tensor("ps_cntr", [128, 1], F32))

    S4_P = S0_END + 48
    V4_CMP = lambda it: V3_TPEV_END + it + 1
    V4_TAU = V3_TPEV_END + NIT + 1
    T4_MM = lambda it: T3_TP_END + it + 1
    S4_TAUH = S4_P + 16
    S4_TAU16 = S4_TAUH + 16
    V4_MASK = V4_TAU + 1
    S4_TOTS = S4_TAU16 + 16
    S4_TOTST = S4_TOTS + 16
    V4_OFFS = V4_MASK + 1
    S4_OFFH = S4_TOTST + 16
    S4_OFFS = S4_OFFH + 16
    V4_CAND = V4_OFFS + 1
    P4_SG = P2_MEMSET + 1
    P4_LIB = P4_SG + 1
    S4_IDX = S4_OFFS + 128 + (16 if DEBUG else 0)
    S4_GH = S4_IDX + 16 * NE
    S4_GCOL = S4_GH + 16 * NE
    S4_END = S4_GCOL

    with nc.Block() as blk:
        @blk.sync
        def _(s):
            s.wait_ge(sPR, 16 * (11 if DEBUG else 1))
            s.dma_start(p128[:], probsh.rearrange("e (q c) -> (e q) c", q=16)).then_inc(sS, 16)
            s.dma_start(p16[:].rearrange("p (e c) -> p e c", e=8),
                        probsh.rearrange("e (p c) -> p e c", p=16)).then_inc(sS, 16)
            s.dma_start(tkp1[:], tokp1).then_inc(sS, 16)

        @blk.vector
        def _(v):
            v.wait_ge(sS, S4_P)
            v.memset(bst[:, 0:1], 0.0)
            v.memset(bst[:, 1:2], 1.0)
            v.memset(bst[:, 2:3], 0.5)
            v.memset(zeros16[:], 0.0)
            v.memset(scrr[:, 0:2], 0.0)
            v.drain()
            v.tensor_scalar(scrr[:, :], p128[:, :], bst[:, 2:3], None, AL.is_ge)
            v.drain()
            v.reduce_sum(cntp[:, :], scrr[:, :], axis=AX.X).then_inc(sV, 1)
            for it in range(1, NIT + 1):
                v.wait_ge(sT, T4_MM(it - 1))
                v.tensor_copy(bst[:, 3:4], ps_cnt[:, :])
                v.drain()
                v.tensor_scalar(selu[:, :], bst[:, 3:4], float(KSEL), None, AL.is_ge)
                v.tensor_scalar(sel2u[:, :], bst[:, 3:4], float(KSEL), None, AL.is_lt)
                v.drain()
                v.copy_predicated(bst[:, 0:1], selu[:, :], bst[:, 2:3])
                v.drain()
                lastcp = v.copy_predicated(bst[:, 1:2], sel2u[:, :], bst[:, 2:3])
                v.drain()
                if it < NIT:
                    v.drain()
                    v.tensor_add(bst[:, 2:3], bst[:, 0:1], bst[:, 1:2])
                    v.drain()
                    v.tensor_scalar(bst[:, 2:3], bst[:, 2:3], 0.5, None, AL.mult)
                    v.drain()
                    v.tensor_scalar(scrr[:, :], p128[:, :], bst[:, 2:3], None, AL.is_ge)
                    v.drain()
                    v.reduce_sum(cntp[:, :], scrr[:, :], axis=AX.X).then_inc(sV, 1)
                else:
                    lastcp.then_inc(sV, 1)

        @blk.tensor
        def _(t):
            for it in range(NIT):
                t.wait_ge(sV, V4_CMP(it))
                t.matmul(ps_cnt[:, :], gml[:, :], cntp[:, :], start=True, stop=True
                         ).then_inc(sT, 1)

        @blk.sync
        def _(s):
            s.wait_ge(sV, V4_TAU)
            s.dma_start(tauh.rearrange("(p one) -> p one", one=1), bst[:, 0:1]).then_inc(sS, 16)
            s.wait_ge(sS, S4_TAUH)
            s.dma_start(tau16[:], tauh.rearrange("(e p) -> p e", p=16),
                        allow_slow_non_contiguous=True).then_inc(sS, 16)

        @blk.vector
        def _(v):
            v.wait_ge(sS, S4_TAU16)
            for e in range(NE):
                sl = slice(256 * e, 256 * (e + 1))
                v.tensor_scalar(maskr[:, sl], p16[:, sl], tau16[:, e:e + 1], None, AL.is_ge)
                v.drain()
                v.tensor_tensor_scan(cumr[:, sl], maskr[:, sl], zeros16[:, :], 0.0, AL.add, AL.max)
                v.drain()
                ins = v.tensor_copy(totsr[:, e:e + 1], cumr[:, 256 * e + 255:256 * e + 256])
                v.drain()
            ins.then_inc(sV, 1)

        @blk.sync
        def _(s):
            s.wait_ge(sV, V4_MASK)
            s.dma_start(totsh, totsr[:]).then_inc(sS, 16)
            s.wait_ge(sS, S4_TOTS)
            s.dma_start(totsT[:], totsh.rearrange("p e -> e p"),
                        allow_slow_non_contiguous=True).then_inc(sS, 16)

        @blk.vector
        def _(v):
            v.wait_ge(sS, S4_TOTST)
            v.memset(zeros16[:], 0.0)
            v.drain()
            v.tensor_tensor_scan(totsTs[:, :], totsT[:, :], zeros16[0:8, 0:16], 0.0, AL.add, AL.max)
            v.drain()
            v.tensor_sub(totsTs[:, :], totsTs[:, :], totsT[:, :]).then_inc(sV, 1)

        @blk.sync
        def _(s):
            s.wait_ge(sV, V4_OFFS)
            s.dma_start(offsh, totsTs[:]).then_inc(sS, 16)
            s.wait_ge(sS, S4_OFFH)
            s.dma_start(offsr[:], offsh.rearrange("e p -> p e"),
                        allow_slow_non_contiguous=True).then_inc(sS, 16)

        @blk.vector
        def _(v):
            v.wait_ge(sS, S4_OFFS)
            for e in range(NE):
                sl = slice(256 * e, 256 * (e + 1))
                v.tensor_scalar(cumr[:, sl], cumr[:, sl], offsr[:, e:e + 1], None, AL.add)
                v.drain()
                v.tensor_scalar(selr[:, sl], cumr[:, sl], float(KSEL), None, AL.is_le)
                v.drain()
                v.tensor_mul(selr[:, sl], selr[:, sl], maskr[:, sl])
                v.drain()
                v.tensor_mul(scr16[:, :], selr[:, sl], tkp1[:, sl])
                v.drain()
                v.tensor_scalar(cand_i[:, sl], scr16[:, :], 1.0, None, AL.subtract)
                v.drain()
                v.tensor_scalar(scr16[:, :], p16[:, sl], 1.0, None, AL.add)
                v.drain()
                v.tensor_mul(scr16[:, :], scr16[:, :], selr[:, sl])
                v.drain()
                ins = v.tensor_scalar(cand_g[:, sl], scr16[:, :], 1.0, None, AL.subtract)
            ins.then_inc(sV, 1)

        @blk.gpsimd
        def _(g):
            g.load_library(sg_lib)
            g.wait_ge(sV, V4_CAND)
            for e in range(NE):
                sl = slice(256 * e, 256 * (e + 1))
                g.sparse_gather(idx16[:, 80 * e:80 * (e + 1)], cand_i[:, sl],
                                num_found=nf[:, 2 * e:2 * e + 1])
                ins = g.sparse_gather(g16[:, 80 * e:80 * (e + 1)], cand_g[:, sl],
                                      num_found=nf[:, 2 * e + 1:2 * e + 2])
            ins.then_inc(sP, 1)
            g.load_library(mlp_lib)

        @blk.sync
        def _(s):
            s.wait_ge(sP, P4_SG)
            for gg in range(8):
                s.dma_start(idx32[16 * gg:16 * (gg + 1), :], idx16[:, :]).then_inc(sS, 16)
            if DEBUG:
                s.wait_ge(sS, S4_OFFS + 128)
                s.dma_start(idxh, idx32[:]).then_inc(sS, 16)
            for e in range(NE):
                s.dma_start(gh[e].rearrange("(c p) -> p c", p=16), g16[:, 80 * e:80 * e + 64],
                            allow_slow_non_contiguous=True).then_inc(sS, 16)
            s.wait_ge(sS, S4_GH)
            for e in range(NE):
                s.dma_start(gcol[:, e, :], gh[e].rearrange("(c p) -> p c", p=128),
                            allow_slow_non_contiguous=True).then_inc(sS, 16)

    ph4.close()
    if STAGE < 5:
        return _fin([st])

    ph5 = ExitStack()
    E5 = ph5.enter_context
    xin = E5(nc.sbuf_tensor("xinf", [128, 1, 2, 8, 512], BF16))
    h1sb = E5(nc.sbuf_tensor("h1sbf", [128, 2, 32, 512], BF16))
    ysb = E5(nc.sbuf_tensor("ysbf", [128, 8, 512], F32))
    w1sb = E5(nc.sbuf_tensor("w1sbf", [128, 2, 8, 512], BF16))
    w2sb = E5(nc.sbuf_tensor("w2sbf", [128, 8, 4, 512], BF16))
    w2f = E5(nc.sbuf_tensor("w2ff", [128, 2, 4, 512], F32))
    ps_h1 = E5(nc.psum_tensor("ps_h1f", [128, 2, 512], F32))
    ps_y = E5(nc.psum_tensor("ps_yf", [128, 2, 512], F32))

    TE = 72 if P5LV >= 2 else 64
    T5 = lambda e: T3_TP_END + NIT + TE * e
    T5_MM1 = lambda e, fg, c, fi: T5(e) + 8 * fg + 4 * c + fi + 1
    T5_MM2 = lambda e, c, sc_: T5(e) + 64 + 4 * c + sc_ + 1
    T5_END = T5(NE)
    C5 = lambda e, g: C3_EXP + 64 * e + g + 1
    C5_CHUNK = lambda e, c: C3_EXP + 64 * e + 56 + 4 * c + 4
    V5_Y = lambda e, c, sc_: V4_CAND + 8 * e + 4 * c + sc_ + 1
    V5_E = lambda e: V4_CAND + 8 * (e + 1)
    D5_GA = lambda e: 32 * (e + 1)
    D5_SC = lambda e: 16 * (e + 1)

    g5base = [0]
    d5base = [0]

    with nc.Block() as blk:
        @blk.gpsimd
        def _(g):
            if P5NE == -1:
                return
            g.wait_ge(sS, S4_END)
            if P5NE == -2:
                return
            for e in range(NE if P5LV >= 1 else P5NE):
                if e >= 1 and P5LV >= 1:
                    g.wait_ge(sT, T5(e - 1) + 64)
                for c in range(2):
                    g.dma_gather(xin[:, 0, c], h2h,
                                 idx32[:, 80 * e + 32 * c:80 * e + 32 * (c + 1)],
                                 512, 512, 1024, elem_step=1024, transpose=True
                                 ).then_inc(sDD, 16)
                g.wait_ge(sDD, 32 * (e + 1))
                if P5LV >= 1:
                    for fg in range(8):
                        gi = 8 * e + fg
                        if gi >= 2:
                            eprev, fgprev = (gi - 2) // 8, (gi - 2) % 8
                            g.wait_ge(sT, T5(eprev) + 8 * fgprev + 8)
                        g.dma_start(w1sb[:, gi % 2],
                                    w1d[e].rearrange("(kc kp) f -> kp kc f", kp=128)
                                    [:, :, 512 * fg:512 * (fg + 1)]).then_inc(sW1[gi % 2], 16)
                if P5LV >= 3:
                    g.wait_ge(sV, V5_E(e))
                    if e >= 1:
                        g.wait_ge(sDS, D5_SC(e - 1))
                    g.dma_scatter_add(scat, ysb[:], idx32[:, 80 * e:80 * e + 64],
                                      1024, 1024, 512, elem_step=512).then_inc(sDS, 16)

        if P5LV >= 1:
            @blk.tensor
            def _(t):
                for e in range(NE):
                    t.wait_ge(sDD, D5_GA(e))
                    for fg in range(8):
                        gi = 8 * e + fg
                        t.wait_ge(sW1[gi % 2], 16 * (gi // 2 + 1))
                        for c in range(2):
                            for fi in range(4):
                                fc = 4 * fg + fi
                                gg = 64 * e + 8 * fg + 4 * c + fi
                                if gg >= 2:
                                    t.wait_ge(sC, C3_EXP + gg - 1)
                                for kc in range(8):
                                    ins = t.matmul(ps_h1[:, gg % 2],
                                                   w1sb[:, gi % 2, kc, 128 * fi:128 * (fi + 1)],
                                                   xin[:, 0, c, kc, :],
                                                   start=(kc == 0), stop=(kc == 7))
                                ins.then_inc(sT, 1)
                    if P5LV >= 2:
                        for c in range(2):
                            t.wait_ge(sC, C5_CHUNK(e, c))
                            t.wait_ge(sWC, 8 * (e + 1))
                            for sc_ in range(4):
                                gy = 8 * e + 4 * c + sc_
                                if gy >= 2:
                                    t.wait_ge(sV, V4_CAND + gy - 1)
                                for fc in range(32):
                                    fg2 = fc // 4
                                    gi2 = 8 * e + fg2
                                    ins = t.matmul(ps_y[:, sc_ % 2],
                                                   h1sb[:, c, fc, 128 * sc_:128 * (sc_ + 1)],
                                                   w2sb[:, fg2, fc % 4, :],
                                                   start=(fc == 0), stop=(fc == 31))
                                ins.then_inc(sT, 1)

            @blk.scalar
            def _(sc):
                for e in range(NE):
                    for fg in range(8):
                        for c in range(2):
                            for fi in range(4):
                                fc = 4 * fg + fi
                                gg = 64 * e + 8 * fg + 4 * c + fi
                                sc.wait_ge(sT, T5_MM1(e, fg, c, fi))
                                sc.activation(h1sb[:, c, fc, :], ps_h1[:, gg % 2], AF.Gelu
                                              ).then_inc(sC, 1)

        if P5LV >= 2:
            @blk.sync
            def _(s):
                for e in range(NE):
                    for fg in range(8):
                        gi2 = 8 * e + fg
                        if gi2 >= 2:
                            s.wait_ge(sWC, gi2 - 1)
                        s.dma_start(w2f[:, gi2 % 2],
                                    w2d[e].rearrange("(fc fp) o -> fp fc o", fp=128)
                                    [:, 4 * fg:4 * (fg + 1), :]).then_inc(sW2F[gi2 % 2], 16)

            @blk.vector
            def _(v):
                for e in range(NE):
                    for fg in range(8):
                        gi2 = 8 * e + fg
                        v.wait_ge(sW2F[gi2 % 2], 16 * (gi2 // 2 + 1))
                        v.tensor_copy(w2sb[:, fg], w2f[:, gi2 % 2]).then_inc(sWC, 1)
                    for c in range(2):
                        for sc_ in range(4):
                            v.wait_ge(sT, T5_MM2(e, c, sc_))
                            if e >= 1 and c == 0 and sc_ == 0 and P5LV >= 3:
                                v.wait_ge(sDS, D5_SC(e - 1))
                            k = 4 * c + sc_
                            v.tensor_scalar(ysb[:, k, :], ps_y[:, sc_ % 2],
                                            gcol[:, e, k:k + 1], None, AL.mult).then_inc(sV, 1)

    ph5.close()
    if STAGE < 6:
        w = {0: [(sDD, max(0, 32 * P5NE))],
             1: [(sC, C3_EXP + 64 * NE)],
             2: [(sV, V4_CAND + 8 * NE)],
             3: [(sDS, 16 * NE)]}[P5LV]
        return _fin([st], waits=w)

    ph6 = ExitStack()
    E6 = ph6.enter_context
    xf = E6(nc.sbuf_tensor("xff", [128, 8, OH], F32))
    sf = E6(nc.sbuf_tensor("sff", [128, 2, OH], F32))
    of = E6(nc.sbuf_tensor("off_", [128, 2, OH], F32))

    V6 = lambda t_: V5_E(NE - 1) + t_ + 1
    SF_ = lambda t_: 16 * (t_ // 2 + 1)

    with nc.Block() as blk:
        @blk.scalar
        def _(sc):
            sc.wait_ge(sX2S[0], 16 * (NCH // 2))
            sc.wait_ge(sX2S[1], 16 * (NCH // 2))
            for t_ in range(NT):
                if t_ >= 8:
                    sc.wait_ge(sV, V6(t_ - 8))
                sc.dma_start(xf[:, t_ % 8], bass.AP(x2h.tensor, t_ * 128 * D, [[D, 128], [1, OH]])
                             ).then_inc(sXF, 16)

        @blk.sync
        def _(s):
            s.wait_ge(sDS, D5_SC(NE - 1))
            s.wait_ge(sS, S4_END)
            for t_ in range(NT):
                if t_ >= 2:
                    s.wait_ge(sV, V6(t_ - 2))
                s.dma_start(sf[:, t_ % 2], scat.rearrange("(n p) c -> n p c", p=128)[t_]
                            ).then_inc(sF[t_ % 2], 16)
                s.wait_ge(sV, V6(t_))
                s.dma_start(yout.rearrange("(n p) c -> n p c", p=128)[t_], of[:, t_ % 2]
                            ).then_inc(sY[t_ % 2], 16)
            s.wait_ge(sY[0], 16 * (NT // 2))
            s.wait_ge(sY[1], 16 * (NT // 2))

        @blk.vector
        def _(v):
            for t_ in range(NT):
                i = t_ % 2
                v.wait_ge(sXF, 16 * (t_ + 1))
                v.wait_ge(sF[t_ % 2], SF_(t_))
                if t_ >= 2:
                    v.wait_ge(sY[t_ % 2], 16 * (t_ // 2))
                    v.drain()
                v.tensor_mul(of[:, i], sf[:, i], gbt[:])
                v.drain()
                v.tensor_add(of[:, i], of[:, i], xf[:, t_ % 8]).then_inc(sV, 1)

    ph6.close()
    st.close()
    nc.compile()
    return nc


_NC_CACHE = [None]


def _consts():
    ident = np.eye(128).astype(ml_dtypes.bfloat16)
    identf = np.eye(128).astype(np.float32)
    col = np.arange(128, dtype=np.float32)
    dtile = np.maximum(col[None, :] - col[:, None] - 1.0, 0.0).astype(np.float32)
    caus = (col[None, :] > col[:, None]).astype(np.float32)
    iotar = np.tile(col[None, :], (128, 1)).astype(np.float32)
    iota127 = (127.0 - col)[:, None].astype(np.float32)
    gmat = np.zeros((128, 128), np.float32)
    for p in range(128):
        gmat[p, (p // 16) * 16:(p // 16) * 16 + 16] = 1.0
    tokp1 = np.zeros((16, 2048), np.float32)
    for e in range(8):
        for p in range(16):
            tokp1[p, 256 * e:256 * (e + 1)] = 256 * p + np.arange(256) + 1
    return {"identc": ident, "identf": identf, "dtile": dtile,
            "caus": caus, "iotar": iotar, "iota127": iota127,
            "gmat": gmat, "tokp1": tokp1}


def _core_inputs(inputs, b, hh):
    f32 = np.float32
    x = inputs["L0_attn_out"][b].astype(f32, copy=False)
    ww = inputs["L0_attn_W_write"].astype(f32, copy=False)
    wr = inputs["L0_attn_W_read"].astype(f32, copy=False)
    dec = inputs["L0_attn_decay"].astype(f32, copy=False)
    la = inputs["L0_attn_log_alpha"].astype(f32, copy=False)
    gw = inputs["L0_ffn_gate_w"].astype(f32, copy=False)
    w1 = inputs["L0_ffn_w1"].astype(f32, copy=False)
    w2 = inputs["L0_ffn_w2"].astype(f32, copy=False)
    cv = inputs["L0_c"][b].astype(f32, copy=False)
    wm = inputs["L0_wmod"].astype(f32, copy=False)
    if hh == 1:
        perm = np.r_[512:1024, 0:512]
        hperm = np.r_[8:16, 0:8]
        x = x[:, perm]
        ww = ww[perm][:, perm]
        wr = wr[perm][:, perm]
        dec = dec[hperm]
        la = la[hperm]
        gw = gw[:, perm]
        w1 = w1[:, perm, :]
        modperm = np.concatenate([r * 1024 + perm for r in range(6)])
        wm = wm[:, modperm]
        w2 = w2[:, :, perm][:, :, 0:OH]
    else:
        w2 = w2[:, :, 0:OH]
    d = {"x": np.ascontiguousarray(x), "wwr": np.ascontiguousarray(ww),
         "wrd": np.ascontiguousarray(wr), "decay": np.ascontiguousarray(dec),
         "logal": np.ascontiguousarray(la), "gatew": np.ascontiguousarray(gw),
         "w1": np.ascontiguousarray(w1), "w2": np.ascontiguousarray(w2),
         "cvec": np.ascontiguousarray(cv), "wmod": np.ascontiguousarray(wm)}
    d.update(_consts())
    return d


def kernel(L0_attn_out, L0_attn_W_write, L0_attn_W_read, L0_attn_decay,
           L0_attn_log_alpha, L0_ffn_gate_w, L0_ffn_w1, L0_ffn_w2, L0_c, L0_wmod):
    inputs = {
        "L0_attn_out": np.asarray(L0_attn_out), "L0_attn_W_write": np.asarray(L0_attn_W_write),
        "L0_attn_W_read": np.asarray(L0_attn_W_read), "L0_attn_decay": np.asarray(L0_attn_decay),
        "L0_attn_log_alpha": np.asarray(L0_attn_log_alpha),
        "L0_ffn_gate_w": np.asarray(L0_ffn_gate_w), "L0_ffn_w1": np.asarray(L0_ffn_w1),
        "L0_ffn_w2": np.asarray(L0_ffn_w2), "L0_c": np.asarray(L0_c),
        "L0_wmod": np.asarray(L0_wmod),
    }
    if _NC_CACHE[0] is None:
        _NC_CACHE[0] = build()
    nc = _NC_CACHE[0]
    in_maps = [_core_inputs(inputs, core // 2, core % 2) for core in range(8)]
    res = run_bass_kernel_spmd(nc, in_maps, core_ids=list(range(8)))
    out = np.zeros((4, T, D), np.float32)
    for core in range(8):
        b, hh = core // 2, core % 2
        sh = res.results[core]["y"]
        if hh == 0:
            out[b][:, 0:OH] = sh
        else:
            out[b][:, OH:D] = sh
    return out
